# version 04
# baseline (speedup 1.0000x reference)
import contextlib
import os
import numpy as np
import ml_dtypes
import concourse.bass as bass
import concourse.mybir as mybir
from concourse.bass_utils import run_bass_kernel_spmd
from concourse.aot_env import allow_ici_all_to_all

os.environ.setdefault("TRN_ALLOW_ICI_ALL_TO_ALL", "1")

F32 = mybir.dt.float32
BF16 = mybir.dt.bfloat16
U32 = mybir.dt.uint32
AF = mybir.ActivationFunctionType
ALU = mybir.AluOpType
AX = mybir.AxisListType

B, T, D, H, HD, II = 2, 8192, 1024, 16, 64, 2816
NC = 8
TOK = B * T // NC
NSTRIP = TOK // 128
WGR = II // NC
WDR = D // NC
NW = D * WGR * 2 + II * WDR
OFF_WU = D * WGR
OFF_WD = 2 * D * WGR
EPS = 1e-5
MIDBITS = [0x3E800000, 0x3F400000, 0x3FA00000, 0x3FE00000,
           0x40200000, 0x40600000, 0x40A00000]
IG = 256
NIG = II // IG
KPH = int(os.environ.get("KPHASE", "3"))
NS3 = int(os.environ.get("KSTRIP", "16"))
NPIECE = 32
NBS = 2 * 16
WTILES = [(0, 0, 128), (0, 1, 128), (0, 2, 96),
          (1, 0, 128), (1, 1, 128), (1, 2, 96),
          (2, 0, 128)]


def gcol(ip):
    r, p = ip // 4, ip % 4
    return (p // 2) * T + 1024 * r + (p % 2) * 512


class Ctr:
    def __init__(self, name):
        self.name, self.sem, self.n = name, None, 0

    def inc(self, inst, k=1):
        if inst is not None and self.sem is not None:
            inst.then_inc(self.sem, k)
        self.n += k
        return self.n


class MockInst:
    def then_inc(self, sem, k=None):
        return self


class MockOps:
    def __getattr__(self, name):
        return lambda *a, **kw: MockInst()


class _MT:
    def __getitem__(self, i):
        return self

    def __call__(self, *a, **kw):
        return self

    def __getattr__(self, name):
        return self


class Ctx:
    def __init__(self, record, EV):
        self.record = record
        self.EV = EV
        self.C = None

    def mark(self, key, ctr):
        if self.record:
            self.EV[key] = (ctr.name, ctr.n)
        else:
            assert self.EV.get(key) == (ctr.name, ctr.n), (
                "EV mismatch", key, self.EV.get(key), ctr.name, ctr.n)

    def w(self, eng, key):
        if not self.record:
            name, val = self.EV[key]
            if val > 0:
                eng.wait_ge(self.C[name].sem, val)

    def wv(self, eng, ctr, val):
        if not self.record and val > 0:
            eng.wait_ge(ctr.sem, val)


def emit_quant(ctx, v, eng, svc, src_ap, dst_ap, P, W, scr):
    nb = W // 16
    xn, sc, iv = scr["xn"], scr["sc"], scr["iv"]
    c1, c2, c3, c4 = scr["c1"], scr["c2"], scr["c3"], scr["c4"]
    svc.inc(v.tensor_reduce(sc[0:P, 0:nb, None],
                            src_ap.rearrange("p (b k) -> p b k", k=16),
                            AX.X, ALU.max, apply_absolute_value=True))
    ctx.wv(eng, svc, svc.n)
    svc.inc(v.tensor_scalar(sc[0:P, 0:nb], sc[0:P, 0:nb], 1.0 / 6.0, 1e-12,
                            ALU.mult, ALU.max))
    ctx.wv(eng, svc, svc.n)
    svc.inc(v.reciprocal(iv[0:P, 0:nb], sc[0:P, 0:nb]))
    ctx.wv(eng, svc, svc.n)
    v.tensor_tensor(xn[0:P, 0:W].rearrange("p (b k) -> p b k", k=16),
                    src_ap.rearrange("p (b k) -> p b k", k=16),
                    iv[0:P, 0:nb, None].broadcast_to([P, nb, 16]), ALU.mult)
    xn_u = xn[0:P, 0:W].bitcast(U32)
    v.tensor_scalar(c4[0:P, 0:W], xn[0:P, 0:W], 0.0, None, ALU.is_lt)
    v.tensor_scalar(c4[0:P, 0:W], c4[0:P, 0:W], -2.0, 1.0, ALU.mult, ALU.add)
    v.tensor_scalar(xn_u, xn_u, 0x7FFFFFFF, None, ALU.bitwise_and)
    v.tensor_scalar(c1[0:P, 0:W], xn_u, MIDBITS[0], None, ALU.is_gt)
    v.tensor_scalar(c2[0:P, 0:W], xn_u, MIDBITS[1], None, ALU.is_gt)
    v.tensor_tensor(c1[0:P, 0:W], c1[0:P, 0:W], c2[0:P, 0:W], ALU.add)
    v.tensor_scalar(c2[0:P, 0:W], xn_u, MIDBITS[2], None, ALU.is_gt)
    v.tensor_scalar(c3[0:P, 0:W], xn_u, MIDBITS[3], None, ALU.is_gt)
    v.tensor_tensor(c2[0:P, 0:W], c2[0:P, 0:W], c3[0:P, 0:W], ALU.add)
    v.tensor_tensor(c1[0:P, 0:W], c1[0:P, 0:W], c2[0:P, 0:W], ALU.add)
    v.tensor_scalar(c1[0:P, 0:W], c1[0:P, 0:W], 0.5, None, ALU.mult)
    v.tensor_scalar(c2[0:P, 0:W], xn_u, MIDBITS[4], None, ALU.is_gt)
    v.tensor_scalar(c3[0:P, 0:W], xn_u, MIDBITS[5], None, ALU.is_gt)
    v.tensor_tensor(c2[0:P, 0:W], c2[0:P, 0:W], c3[0:P, 0:W], ALU.add)
    v.tensor_tensor(c1[0:P, 0:W], c1[0:P, 0:W], c2[0:P, 0:W], ALU.add)
    v.tensor_scalar(c2[0:P, 0:W], xn_u, MIDBITS[6], None, ALU.is_gt)
    v.tensor_scalar(c2[0:P, 0:W], c2[0:P, 0:W], 2.0, None, ALU.mult)
    v.tensor_tensor(c1[0:P, 0:W], c1[0:P, 0:W], c2[0:P, 0:W], ALU.add)
    v.tensor_tensor(c1[0:P, 0:W], c1[0:P, 0:W], c4[0:P, 0:W], ALU.mult)
    return v.tensor_tensor(dst_ap.rearrange("p (b k) -> p b k", k=16),
                           c1[0:P, 0:W].rearrange("p (b k) -> p b k", k=16),
                           sc[0:P, 0:nb, None].broadcast_to([P, nb, 16]),
                           ALU.mult)


def build_nc(debug=False):
    nc = bass.Bass("TRN2", target_bir_lowering=False)

    x_h = nc.dram_tensor("x_own", [TOK, D], F32, kind="ExternalInput")
    wqk_h = nc.dram_tensor("wqk", [D, 256], F32, kind="ExternalInput")
    wv_h = nc.dram_tensor("wv", [D, 128], F32, kind="ExternalInput")
    bqk_h = nc.dram_tensor("bqk", [1, 256], F32, kind="ExternalInput")
    bv_h = nc.dram_tensor("bv", [1, 128], F32, kind="ExternalInput")
    wout_h = nc.dram_tensor("wout", [D, D], F32, kind="ExternalInput")
    bout_h = nc.dram_tensor("bout", [1, D], F32, kind="ExternalInput")
    wg_h = nc.dram_tensor("wg_sl", [WGR, D], F32, kind="ExternalInput")
    wu_h = nc.dram_tensor("wu_sl", [WGR, D], F32, kind="ExternalInput")
    wd_h = nc.dram_tensor("wd_sl", [WDR, II], F32, kind="ExternalInput")
    ctab_h = nc.dram_tensor("ctab", [128, 2048], F32, kind="ExternalInput")
    stab_h = nc.dram_tensor("stab", [128, 2048], F32, kind="ExternalInput")
    mask_h = nc.dram_tensor("mask01", [128, 4, 512], F32, kind="ExternalInput")
    id_h = nc.dram_tensor("ident", [128, 128], F32, kind="ExternalInput")
    e2_h = nc.dram_tensor("e2", [128, 2], F32, kind="ExternalInput")
    one_h = nc.dram_tensor("ones_in", [1, 512], F32, kind="ExternalInput")
    out_h = nc.dram_tensor("out", [TOK, D], F32, kind="ExternalOutput")

    xlnT_in = nc.dram_tensor("xlnT_in", [D, TOK], BF16)
    xlnT_all = nc.dram_tensor("xlnT_all", [NC * D, TOK], BF16)
    qT_dram = nc.dram_tensor("qT_dram", [128, B * T], BF16)
    attT_in = nc.dram_tensor("attT_in", [NC * 128, TOK], BF16)
    attT_out = nc.dram_tensor("attT_out", [NC * 128, TOK], BF16)
    ccw_in = nc.dram_tensor("ccw_in", [NW], BF16)
    ccw_all = nc.dram_tensor("ccw_all", [NC, NW], BF16)
    nm_dram = nc.dram_tensor("nm_dram", [8], F32)
    zscr = nc.dram_tensor("zscr", [1024], BF16)

    dbg = {}
    if debug:
        dbg["xlnT"] = nc.dram_tensor("d_xlnT", [D, TOK], BF16, kind="ExternalOutput")
        dbg["qT"] = nc.dram_tensor("d_qT", [128, B * T], BF16, kind="ExternalOutput")
        dbg["att"] = nc.dram_tensor("d_att", [NC * 128, TOK], BF16, kind="ExternalOutput")
        dbg["negm"] = nc.dram_tensor("d_negm", [128, 4], F32, kind="ExternalOutput")
        dbg["ccw"] = nc.dram_tensor("d_ccw", [NW], BF16, kind="ExternalOutput")

    esg = contextlib.ExitStack()
    es12 = contextlib.ExitStack()
    es0 = contextlib.ExitStack()
    es3 = contextlib.ExitStack()
    es_ps1 = contextlib.ExitStack()
    es_ps2 = contextlib.ExitStack()

    def sb(es, name, shape, dt):
        return es.enter_context(nc.sbuf_tensor(name, shape, dt))

    def pst(es, name, shape, dt=F32):
        return es.enter_context(nc.psum_tensor(name, shape, dt))

    ident = sb(esg, "ident_s", [128, 128], BF16)
    ones = sb(esg, "ones_s", [1, 512], BF16)
    e2 = sb(esg, "e2_s", [128, 2], BF16)
    wout = sb(esg, "wout_s", [128, 8, D], BF16)
    bout = sb(esg, "bout_s", [1, D], BF16)
    negm = sb(esg, "negm_s", [128, 4], F32)
    small = sb(esg, "small_s", [1, 16], F32)
    epsb = sb(esg, "epsb_s", [128, 1], F32)
    smbf = sb(esg, "smbf_s", [1, 8], BF16)

    semnames = ["sd", "sg", "scc", "stn", "ssc", "svc",
                "s_ws", "s_x0", "s_x1", "s_xt0", "s_xt1", "s_q0", "s_q1",
                "s_att", "s_x3", "sgw", "sg_c0", "sg_c1", "sg_q",
                "sg_a0", "sg_a1", "sg_z"]
    handles = {n: esg.enter_context(nc.semaphore(n)) for n in semnames}
    EV = {}

    wqk = sb(es12, "wqk_s", [128, 8, 256], BF16)
    wv = sb(es12, "wv_s", [128, 8, 128], BF16)
    bqk = sb(es12, "bqk_s", [1, 256], BF16)
    bv = sb(es12, "bv_s", [1, 128], BF16)
    mask = sb(es12, "mask_s", [128, 4, 512], BF16)
    ctab = sb(es12, "ctab_s", [128, 2048], F32)
    stab = sb(es12, "stab_s", [128, 2048], F32)
    xt = [sb(es12, f"xt{i}", [128, 8, 512], BF16) for i in range(2)]
    qTb = [sb(es12, f"qTb{i}", [128, 512], BF16) for i in range(2)]
    kT = sb(es12, "kT", [128, B * T], BF16)
    V = sb(es12, "V", [128, 128, 130], BF16)
    rop = [sb(es12, f"rop{i}", [32, 512], F32) for i in range(2)]
    sqb = sb(es12, "sqb", [128, 512], BF16)
    nmq = sb(es12, "nmq", [2, 32], F32)
    nmk = sb(es12, "nmk", [2, 32], F32)
    nred = sb(es12, "nred", [2, 4], F32)
    xbuf = [sb(es0, f"xbuf{i}", [128, D], F32) for i in range(2)]
    xln = [sb(es0, f"xln{i}", [128, D], BF16) for i in range(2)]
    bn12 = sb(es0, "bn12", [128, 12], F32)
    bn2 = sb(es0, "bn2", [128, 2], F32)
    stdv = sb(es0, "stdv", [128, 1], F32)
    rstd = sb(es0, "rstd", [128, 1], F32)
    cstrip = [sb(es0, f"cstrip{i}", [128, 8, 128], BF16) for i in range(2)]
    QW = 1408
    SCR0 = {
        "xn": sb(es0, "q_xn", [128, QW], F32),
        "sc": sb(es0, "q_sc", [128, QW // 16], F32),
        "iv": sb(es0, "q_iv", [128, QW // 16], F32),
        "c1": sb(es0, "q_c1", [128, QW], BF16),
        "c2": sb(es0, "q_c2", [128, QW], BF16),
        "c3": sb(es0, "q_c3", [128, QW], BF16),
        "c4": sb(es0, "q_c4", [128, QW], BF16),
    }
    wslice = sb(es0, "wslice", [128, II], F32)
    wqrow = sb(es0, "wqrow", [128, II], BF16)
    wT_out = sb(es0, "wT_out", [128, 8, WGR], BF16)
    wdT_out = sb(es0, "wdT_out", [128, 22, 128], BF16)
    pqk = [pst(es_ps1, f"pqk{i}", [128, 512]) for i in range(2)]
    pv = [pst(es_ps1, f"pv{i}", [128, 128]) for i in range(2)]
    pn = pst(es_ps1, "pn", [2, 512])
    ptr = [pst(es_ps1, f"ptr{i}", [128, 128], BF16) for i in range(2)]


    def f_sync(ctx, E, C):
        def d(ctr, dst, src):
            return ctr.inc(E.dma_start(out=dst, in_=src), 16)

        d(C["s_ws"], ctab[:], ctab_h.ap())
        d(C["s_ws"], stab[:], stab_h.ap())
        for i, (wi, ti, P) in enumerate(WTILES):
            ctx.w(E, f"wsl_freeB{i}")
            wh = [wg_h, wu_h, wd_h][wi]
            WR = II if wi == 2 else D
            d(C["s_ws"], wslice[0:P, 0:WR], wh.ap()[128 * ti:128 * ti + P, :])
            ctx.mark(f"ld_w{i}", C["s_ws"])
        for s in range(NSTRIP):
            if s >= 2:
                ctx.w(E, f"ln_apply{s - 2}")
            d(C[f"s_x{s % 2}"], xbuf[s % 2][:], x_h.ap()[128 * s:128 * (s + 1), :])
            ctx.mark(f"ld_x{s}", C[f"s_x{s % 2}"])
        if KPH < 2:
            return
        for ip in range(NPIECE):
            r, p = ip // 4, ip % 4
            if ip == 0:
                ctx.wv(E, C["scc"], 2)
            if ip >= 2:
                ctx.w(E, f"xt_used{ip - 2}")
            d(C[f"s_xt{ip % 2}"], xt[ip % 2][:],
              xlnT_all.ap()[D * r:D * (r + 1), 512 * p:512 * (p + 1)]
              .rearrange("(c p) t -> p c t", p=128))
            ctx.mark(f"ld_xt{ip}", C[f"s_xt{ip % 2}"])
    def f_sync2(ctx, E, C, B2):
        def d(ctr, dst, src):
            return ctr.inc(E.dma_start(out=dst, in_=src), 16)

        for bsi in range(NBS):
            b, s = bsi // 16, bsi % 16
            if bsi == 0:
                ctx.wv(E, C["sg_q"], 16 * NPIECE)
            if bsi >= 2:
                ctx.w(E, f"s_done{bsi - 2}")
            d(C[f"s_q{bsi % 2}"], B2["qstrip"][bsi % 2][:],
              qT_dram.ap()[:, b * T + 512 * s: b * T + 512 * (s + 1)])
            ctx.mark(f"ld_q{bsi}", C[f"s_q{bsi % 2}"])

    def f_sync3(ctx, E, C, bufs):
        def d(ctr, dst, src):
            return ctr.inc(E.dma_start(out=dst, in_=src), 16)

        for so in range(NS3):
            if so == 0:
                ctx.wv(E, C["scc"], 3)
                ctx.w(E, "ph2_tn")
                ctx.w(E, "ph2_sv")
                ctx.w(E, "ph2_sc")
            if so >= 1:
                ctx.w(E, f"po{so - 1}")
            d(C["s_att"], bufs["att_t"][:],
              attT_out.ap().rearrange("(r p) t -> p r t", p=128)
              [:, :, 128 * so:128 * (so + 1)])
            ctx.mark(f"ld_att{so}", C["s_att"])
            if so >= 1:
                ctx.w(E, f"outdma{so - 1}")
            d(C["s_x3"], bufs["xbuf3"][:], x_h.ap()[128 * so:128 * (so + 1), :])
            ctx.mark(f"ld_x3_{so}", C["s_x3"])

    def f_gpsimd(ctx, E, C):
        sg, sgw, scc = C["sg"], C["sgw"], C["scc"]

        def d(ctr, dst, src):
            return ctr.inc(E.dma_start(out=dst, in_=src), 16)

        d(sgw, ident[:], id_h.ap())
        d(sgw, ones[:], one_h.ap())
        d(sgw, e2[:], e2_h.ap())
        d(sgw, mask[:], mask_h.ap())
        d(sgw, wqk[:], wqk_h.ap().rearrange("(c p) n -> p c n", p=128))
        d(sgw, wv[:], wv_h.ap().rearrange("(c p) n -> p c n", p=128))
        d(sgw, bqk[:], bqk_h.ap())
        d(sgw, bv[:], bv_h.ap())
        d(sgw, bout[:], bout_h.ap())
        d(sgw, wout[:], wout_h.ap().rearrange("(c p) n -> p c n", p=128))
        ctx.mark("w_bf16", sgw)
        ctx.mark("consts", sgw)
        ctx.w(E, "wT_wg_done")
        d(sg, ccw_in.ap()[0:OFF_WU].rearrange("(c p i) -> p c i", p=128, i=WGR),
          wT_out[:])
        ctx.mark("ccw_wg_dma", sg)
        ctx.w(E, "wT_wu_done")
        d(sg, ccw_in.ap()[OFF_WU:OFF_WD].rearrange("(c p i) -> p c i", p=128, i=WGR),
          wT_out[:])
        ctx.mark("ccw_wu_dma", sg)
        ctx.w(E, "wT_wd_done")
        d(sg, ccw_in.ap()[OFF_WD:NW].rearrange("(c p h) -> p c h", p=128, h=128),
          wdT_out[:])
        ctx.wv(E, sg, sg.n)
        scc.inc(E.collective_compute(
            "AllGather", ALU.bypass, replica_groups=[list(range(NC))],
            ins=[ccw_in.ap().opt()], outs=[ccw_all.ap().opt()]))
        for s in range(NSTRIP):
            ctx.w(E, f"xtev{s * 8 + 7}")
            d(C[f"sg_c{s % 2}"], xlnT_in.ap().rearrange("(c p) t -> p c t", p=128)
              [:, :, 128 * s:128 * (s + 1)], cstrip[s % 2][:])
            ctx.mark(f"cdma{s}", C[f"sg_c{s % 2}"])
        ctx.wv(E, C["sg_c0"], C["sg_c0"].n)
        ctx.wv(E, C["sg_c1"], C["sg_c1"].n)
        scc.inc(E.collective_compute(
            "AllGather", ALU.bypass, replica_groups=[list(range(NC))],
            ins=[xlnT_in.ap().opt()], outs=[xlnT_all.ap().opt()]))
        if KPH < 2:
            return
        for ip in range(NPIECE):
            ctx.w(E, f"rope{ip}_0")
            g = gcol(ip)
            d(C["sg_q"], qT_dram.ap()[:, g:g + 512], qTb[ip % 2][:])
            ctx.mark(f"qdma{ip}", C["sg_q"])
        ctx.w(E, "nred_done")
        sg.inc(E.dma_start(out=nm_dram.ap()[0:4].rearrange("(b h) -> h b", h=2),
                           in_=nred[0:2, 0:2],
                           allow_slow_non_contiguous=True), 16)
        sg.inc(E.dma_start(out=nm_dram.ap()[4:8].rearrange("(b h) -> h b", h=2),
                           in_=nred[0:2, 2:4],
                           allow_slow_non_contiguous=True), 16)
        ctx.wv(E, sg, sg.n)
        d(sg, small[0:1, 0:8], nm_dram.ap()[None, :])
        ctx.mark("nm_back", sg)
    def f_gpsimd2(ctx, E, C, B2):
        scc = C["scc"]

        def d(ctr, dst, src):
            return ctr.inc(E.dma_start(out=dst, in_=src), 16)

        sgz = C["sg_z"]
        for bsi in range(NBS):
            b, s = bsi // 16, bsi % 16
            o = s // 2
            oc = b * 1024 + (s % 2) * 512
            ctx.w(E, f"zi{bsi}")
            d(sgz, zscr.ap()[None, :], B2["zib"][:])
            ctx.wv(E, sgz, sgz.n)
            d(sgz, B2["zbs"][:],
              zscr.ap()[None, :].broadcast_to([64, 1024]))
            ctx.w(E, f"attmul{bsi}")
            d(C[f"sg_a{bsi % 2}"], attT_in.ap()[128 * o:128 * (o + 1), oc:oc + 512]
              .rearrange("(j p) t -> p j t", p=64),
              B2["attb"][bsi % 2][:].rearrange("p (j t) -> p j t", j=2))
            ctx.mark(f"attdma{bsi}", C[f"sg_a{bsi % 2}"])
        ctx.wv(E, C["sg_a0"], C["sg_a0"].n)
        ctx.wv(E, C["sg_a1"], C["sg_a1"].n)
        with allow_ici_all_to_all():
            scc.inc(E.collective_compute(
                "AllToAll", ALU.bypass, replica_groups=[list(range(NC))],
                ins=[attT_in.ap().opt()], outs=[attT_out.ap().opt()]))

    def f_gpsimd3(ctx, E, C, bufs):
        sg = C["sg"]

        def d(dst, src):
            return sg.inc(E.dma_start(out=dst, in_=src), 16)

        ctx.wv(E, C["sg_a0"], C["sg_a0"].n)
        ctx.wv(E, C["sg_a1"], C["sg_a1"].n)
        ctx.wv(E, C["scc"], 3)
        ctx.w(E, "ph2_tn")
        ctx.w(E, "ph2_sv")
        ctx.w(E, "ph2_sc")
        for c in range(8):
            d(bufs["wg_sb"][:, c, :].rearrange("p (r i) -> p r i", i=WGR),
              ccw_all.ap()[:, 128 * c * WGR:(128 * c + 128) * WGR]
              .rearrange("r (p i) -> p r i", p=128))
            d(bufs["wu_sb"][:, c, :].rearrange("p (r i) -> p r i", i=WGR),
              ccw_all.ap()[:, OFF_WU + 128 * c * WGR:
                           OFF_WU + (128 * c + 128) * WGR]
              .rearrange("r (p i) -> p r i", p=128))
        for ic in range(22):
            d(bufs["wd_sb"][:, ic, :].rearrange("p (r h) -> p r h", h=128),
              ccw_all.ap()[:, OFF_WD + 128 * ic * 128:
                           OFF_WD + (128 * ic + 128) * 128]
              .rearrange("r (p h) -> p r h", p=128))
        ctx.mark("ffnw", sg)
        for so in range(NSTRIP):
            if so < NS3:
                ctx.w(E, f"outadd{so}")
            d(out_h.ap()[128 * so:128 * (so + 1), :], bufs["xbuf3"][:])
            if so < NS3:
                ctx.mark(f"outdma{so}", sg)
        if debug:
            d(dbg["xlnT"].ap(), xlnT_in.ap())
            d(dbg["qT"].ap(), qT_dram.ap())
            d(dbg["att"].ap(), attT_in.ap())
            d(dbg["negm"].ap(), negm[:])
            d(dbg["ccw"].ap(), ccw_in.ap())
        ctx.wv(E, sg, sg.n)
        ctx.wv(E, C["sgw"], C["sgw"].n)
        ctx.wv(E, C["sg_q"], C["sg_q"].n)

    def f_tensor(ctx, E, ops, C):
        stn = C["stn"]
        trj = 0
        trev = []
        ctx.w(E, "consts")
        for i, (wi, ti, P) in enumerate(WTILES):
            ctx.w(E, f"wq_q{i}")
            nch = (II if wi == 2 else D) // 128
            for ci in range(nch):
                if trj >= 2:
                    ctx.w(E, trev[trj - 2])
                inst = ops.transpose(ptr[trj % 2][0:128, 0:P],
                                     wqrow[0:P, 128 * ci:128 * (ci + 1)],
                                     ident[0:P, 0:P])
                stn.inc(inst)
                ctx.mark(f"wtd{i}_{ci}", stn)
                trev.append(f"wtev{i}_{ci}")
                trj += 1
            ctx.mark(f"wq_used{i}", stn)
        for s in range(NSTRIP):
            ctx.w(E, f"ln_apply{s}")
            for ci in range(8):
                j = s * 8 + ci
                if trj >= 2:
                    ctx.w(E, trev[trj - 2])
                inst = ops.transpose(ptr[trj % 2][:],
                                     xln[s % 2][:, 128 * ci:128 * (ci + 1)],
                                     ident[:])
                stn.inc(inst)
                ctx.mark(f"xt_tr{j}", stn)
                trev.append(f"xtev{j}")
                trj += 1
        if KPH < 2:
            return
        for ip in range(NPIECE):
            ctx.w(E, f"ld_xt{ip}")
            if ip == 0:
                ctx.w(E, "w_bf16")
            for cc in range(2):
                if ip >= 1:
                    ctx.w(E, f"rope{ip - 1}_{cc}")
                inst = ops.matmul(pqk[cc][:], bqk[0:1, 128 * cc:128 * (cc + 1)],
                                  ones[0:1, :], start=True, stop=False)
                for c in range(8):
                    inst = ops.matmul(pqk[cc][:],
                                      wqk[:, c, 128 * cc:128 * (cc + 1)],
                                      xt[ip % 2][:, c, :],
                                      start=False, stop=(c == 7))
                stn.inc(inst)
                ctx.mark(f"qk{ip}_{cc}", stn)
            for t4 in range(4):
                k = ip * 4 + t4
                if k >= 2:
                    ctx.w(E, f"vev{k - 2}")
                inst = ops.matmul(pv[t4 % 2][:], ones[0:1, 0:128], bv[0:1, :],
                                  start=True, stop=False)
                for c in range(8):
                    inst = ops.matmul(pv[t4 % 2][:],
                                      xt[ip % 2][:, c, 128 * t4:128 * (t4 + 1)],
                                      wv[:, c, :], start=False, stop=(c == 7))
                stn.inc(inst)
                ctx.mark(f"v{ip}_{t4}", stn)
            ctx.mark(f"xt_used{ip}", stn)
            for cc in range(2):
                ctx.w(E, f"sq{ip}_{cc}")
                if ip + cc > 0:
                    pip, pcc = (ip, 0) if cc == 1 else (ip - 1, 1)
                    ctx.w(E, f"nred{pip}_{pcc}")
                inst = ops.matmul(pn[:], e2[:], sqb[:], start=True, stop=True)
                stn.inc(inst)
                ctx.mark(f"nmm{ip}_{cc}", stn)
        ctx.w(E, "negm_bf")
        inst = ops.matmul(pqk[0][0:128, 0:4], ones[0:1, 0:128],
                          smbf[0:1, 0:4], start=True, stop=True)
        stn.inc(inst)
        ctx.mark("negm_mm", stn)

    def f_tensor2(ctx, E, ops, C, P2, B2):
        stn = C["stn"]
        sps, avp = P2["sps"], P2["avp"]
        gun = 0
        for bsi in range(NBS):
            b, s = bsi // 16, bsi % 16
            nk = 4 * (s + 1)
            nblk = (nk + 7) // 8
            ctx.w(E, f"ld_q{bsi}")
            if bsi == 0:
                ctx.w(E, "negm_sb")
                ctx.w(E, "negm_bf")
            for blk in range(nblk):
                c0 = blk * 8
                cn = min(8, nk - c0)
                for jp in range(cn // 2):
                    pr = c0 // 2 + jp
                    for h in range(2):
                        if gun >= 2:
                            ctx.w(E, f"expu{gun - 2}")
                        for k2 in range(2):
                            kc = 2 * pr + k2
                            inst = ops.matmul(
                                sps[gun % 2][:, 512 * k2:512 * (k2 + 1)],
                                kT[64 * h:64 * (h + 1),
                                   b * T + 128 * kc:b * T + 128 * (kc + 1)],
                                B2["qstrip"][bsi % 2][64 * h:64 * (h + 1), :],
                                start=True, stop=True,
                                tile_position=(64 * h, 0))
                        stn.inc(inst)
                        ctx.mark(f"su{gun}", stn)
                        gun += 1
                for j in range(cn):
                    kc = c0 + j
                    ua = gun - cn + (j // 2) * 2
                    ctx.w(E, f"expu{ua}")
                    ctx.w(E, f"expu{ua + 1}")
                    if kc >= nk - 4:
                        ctx.w(E, f"mask{bsi}_{kc}")
                    if kc == 0 and bsi >= 2:
                        ctx.w(E, f"attmul{bsi - 2}")
                    ci = (b * T) // 128 + kc
                    for h in range(2):
                        inst = ops.matmul(avp[h][bsi % 2][:],
                                          V[:, ci, 65 * h:65 * h + 65],
                                          B2["pT"][h][blk % 2][:, j, :],
                                          start=(kc == 0), stop=(kc == nk - 1))
                    if j == cn - 1:
                        stn.inc(inst)
                        ctx.mark(f"avb{bsi}_{blk}", stn)
            ctx.mark(f"s_done{bsi}", stn)
        ctx.mark("ph2_tn", stn)

    def f_tensor3(ctx, E, ops, C, bufs, P3):
        stn = C["stn"]
        po, dn, tr3, gup = P3["po"], P3["dn"], P3["tr3"], P3["gup"]
        att_t, xq, xqT = bufs["att_t"], bufs["xq"], bufs["xqT"]
        hq_sb, hqT = bufs["hq_sb"], bufs["hqT"]
        wg_sb, wu_sb, wd_sb = bufs["wg_sb"], bufs["wu_sb"], bufs["wd_sb"]
        ctx.w(E, "ffnw")
        trg = []
        gidx = 0

        def tr_group(srcs, mark_key):
            nonlocal gidx
            if len(trg) >= 2:
                ctx.w(E, trg[-2])
            tens = tr3[gidx % 2]
            inst = None
            for k, s_ap in enumerate(srcs):
                inst = ops.transpose(tens[:, k, :], s_ap, ident[:])
            stn.inc(inst)
            ctx.mark(mark_key, stn)
            gidx += 1

        for it in range(NS3 + 1):
            prev, cur = it - 1, it
            if cur < NS3:
                ctx.w(E, f"ld_att{cur}")
                if cur >= 1:
                    ctx.w(E, f"x1_{cur - 1}")
                for half in range(2):
                    inst = ops.matmul(po[half][:], ones[0:1, 0:128],
                                      bout[0:1, 512 * half:512 * (half + 1)],
                                      start=True, stop=False)
                    for r in range(8):
                        inst = ops.matmul(po[half][:], att_t[:, r, :],
                                          wout[:, r, 512 * half:512 * (half + 1)],
                                          start=False, stop=(r == 7))
                stn.inc(inst)
                ctx.mark(f"po{cur}", stn)
            if prev >= 0:
                ctx.w(E, f"hq{prev}")
                for g in range(6):
                    ics = range(4 * g, min(4 * g + 4, 22))
                    tr_group([hq_sb[prev % 2][:, 128 * ic:128 * (ic + 1)]
                              for ic in ics], f"hqtrg{prev}_{g}")
                    trg.append(f"hqevg{prev}_{g}")
                ctx.w(E, f"hqevg{prev}_5")
                ctx.w(E, f"x1_{prev}")
                if prev >= 1:
                    ctx.w(E, f"outadd{prev - 1}")
                for ic in range(22):
                    for half in range(2):
                        inst = ops.matmul(dn[half][:], hqT[:, ic, :],
                                          wd_sb[:, ic,
                                                512 * half:512 * (half + 1)],
                                          start=(ic == 0), stop=(ic == 21))
                stn.inc(inst)
                ctx.mark(f"down{prev}", stn)
            if cur < NS3:
                ctx.w(E, f"xq{cur}")
                for g in range(2):
                    tr_group([xq[:, 128 * ci:128 * (ci + 1)]
                              for ci in range(4 * g, 4 * g + 4)],
                             f"xqtrg{cur}_{g}")
                    trg.append(f"xqevg{cur}_{g}")
                ctx.w(E, f"xqevg{cur}_1")
                for ig in range(NIG):
                    if ig >= 2:
                        ctx.w(E, f"hm{cur}_{ig - 2}")
                    elif cur >= 1:
                        ctx.w(E, f"hm{cur - 1}_{10 if ig == 0 else 9}")
                    sl = slice(IG * ig, IG * (ig + 1))
                    for c in range(8):
                        ops.matmul(gup[ig % 2][:, 0:256], xqT[:, c, :],
                                   wg_sb[:, c, sl], start=(c == 0), stop=False)
                        inst = ops.matmul(gup[ig % 2][:, 256:512],
                                          xqT[:, c, :], wu_sb[:, c, sl],
                                          start=False, stop=(c == 7))
                    stn.inc(inst)
                    ctx.mark(f"gu{cur}_{ig}", stn)

    def f_scalar(ctx, E, ops, C):
        ssc = C["ssc"]
        for i, (wi, ti, P) in enumerate(WTILES):
            nch = (II if wi == 2 else D) // 128
            if (wi, ti) == (1, 0):
                ctx.w(E, "ccw_wg_dma")
            for ci in range(nch):
                ctx.w(E, f"wtd{i}_{ci}")
                if wi < 2:
                    dst = wT_out[:, ci, 128 * ti:128 * ti + P]
                else:
                    dst = wdT_out[:, ci, 0:128]
                inst = ops.copy(dst, ptr[ci % 2][0:128, 0:P])
                ssc.inc(inst)
                ctx.mark(f"wtev{i}_{ci}", ssc)
            if (wi, ti) == (0, 2):
                ctx.mark("wT_wg_done", ssc)
            if (wi, ti) == (1, 2):
                ctx.mark("wT_wu_done", ssc)
            if wi == 2:
                ctx.mark("wT_wd_done", ssc)
        ctx.w(E, "epsb")
        for s in range(NSTRIP):
            ctx.w(E, f"bn{s}")
            inst = ops.activation(stdv[:], bn2[:, 1:2], AF.Sqrt, bias=epsb[:])
            ssc.inc(inst)
            ctx.mark(f"sqrt{s}", ssc)
            if s >= 2:
                ctx.w(E, f"cdma{s - 2}")
            for ci in range(8):
                j = s * 8 + ci
                ctx.w(E, f"xt_tr{j}")
                inst = ops.copy(cstrip[s % 2][:, ci, :], ptr[j % 2][:])
                ssc.inc(inst)
                ctx.mark(f"xtev{j}", ssc)
        ctx.mark("xln_done", ssc)
        if KPH < 2:
            return
        for ip in range(NPIECE):
            for t4 in range(4):
                k = ip * 4 + t4
                ctx.w(E, f"v{ip}_{t4}")
                ci = gcol(ip) // 128 + t4
                ops.copy(V[:, ci, 0:64], pv[t4 % 2][:, 0:64])
                inst = ops.copy(V[:, ci, 65:129], pv[t4 % 2][:, 64:128])
                ssc.inc(inst)
                ctx.mark(f"vev{k}", ssc)
        ctx.w(E, "nm_back")
        inst = ops.activation(small[0:1, 0:8], small[0:1, 0:8], AF.Sqrt)
        ssc.inc(inst)
        ctx.mark("nm_sqrt", ssc)
        ctx.w(E, "negm_mm")
        inst = ops.copy(negm[:], pqk[0][0:128, 0:4])
        ssc.inc(inst)
        ctx.mark("negm_sb", ssc)

    def f_scalar2(ctx, E, ops, C, P2, B2):
        ssc = C["ssc"]
        sps = P2["sps"]
        gun = 0
        last_pT = {0: None, 1: None}
        for bsi in range(NBS):
            b, s = bsi // 16, bsi % 16
            nk = 4 * (s + 1)
            nblk = (nk + 7) // 8
            for blk in range(nblk):
                c0 = blk * 8
                cn = min(8, nk - c0)
                if last_pT[blk % 2] is not None:
                    ctx.w(E, last_pT[blk % 2])
                for jp in range(cn // 2):
                    for h in range(2):
                        ctx.w(E, f"su{gun}")
                        inst = ops.activation(
                            B2["pT"][h][blk % 2][:, 2 * jp:2 * jp + 2, :],
                            sps[gun % 2][:], AF.Exp,
                            bias=negm[:, 2 * b + h:2 * b + h + 1],
                            scale=0.125)
                        ssc.inc(inst)
                        ctx.mark(f"expu{gun}", ssc)
                        gun += 1
                last_pT[blk % 2] = f"avb{bsi}_{blk}"
        ctx.mark("ph2_sc", ssc)

    def f_scalar3(ctx, E, ops, C, bufs, P3):
        ssc = C["ssc"]
        tr3, gup = P3["tr3"], P3["gup"]
        gidx = 0
        for it in range(NS3 + 1):
            prev, cur = it - 1, it
            if prev >= 0:
                for g in range(6):
                    n = 4 if g < 5 else 2
                    ctx.w(E, f"hqtrg{prev}_{g}")
                    inst = ops.copy(
                        bufs["hqT"][:, 4 * g:4 * g + n, :],
                        tr3[gidx % 2][:, 0:n, :])
                    gidx += 1
                    ssc.inc(inst)
                    ctx.mark(f"hqevg{prev}_{g}", ssc)
            if cur < NS3:
                ctx.w(E, f"bn3_{cur}")
                inst = ops.activation(bufs["stdv3"][:], bufs["bn2b"][:, 1:2],
                                      AF.Sqrt, bias=epsb[:])
                ssc.inc(inst)
                ctx.mark(f"sqrt3_{cur}", ssc)
                for g in range(2):
                    ctx.w(E, f"xqtrg{cur}_{g}")
                    inst = ops.copy(bufs["xqT"][:, 4 * g:4 * g + 4, :],
                                    tr3[gidx % 2][:, :, :])
                    gidx += 1
                    ssc.inc(inst)
                    ctx.mark(f"xqevg{cur}_{g}", ssc)
                for ig in range(NIG):
                    ctx.w(E, f"gu{cur}_{ig}")
                    if ig >= 2:
                        ctx.w(E, f"hm{cur}_{ig - 2}")
                    inst = ops.activation(bufs["gs"][ig % 2][:],
                                          gup[ig % 2][:, 0:256], AF.Silu)
                    ssc.inc(inst)
                    ctx.mark(f"silu{cur}_{ig}", ssc)

    def f_vector(ctx, E, ops, C):
        svc = C["svc"]
        inst = ops.memset(epsb[:], EPS)
        svc.inc(inst)
        ctx.mark("epsb", svc)
        ops.memset(V[:, :, 64:65], 1.0)
        inst = ops.memset(V[:, :, 129:130], 1.0)
        svc.inc(inst)
        ctx.mark("v_ones", svc)
        for i, (wi, ti, P) in enumerate(WTILES):
            ctx.w(E, f"ld_w{i}")
            if i > 0:
                ctx.w(E, f"wq_used{i - 1}")
            if wi == 2:
                for hh in range(2):
                    inst = emit_quant(
                        ctx, ops, E, svc,
                        wslice[0:P, 1408 * hh:1408 * (hh + 1)],
                        wqrow[0:P, 1408 * hh:1408 * (hh + 1)], P, 1408, SCR0)
                    svc.inc(inst)
            else:
                inst = emit_quant(ctx, ops, E, svc, wslice[0:P, 0:D],
                                  wqrow[0:P, 0:D], P, D, SCR0)
                svc.inc(inst)
            ctx.mark(f"wq_q{i}", svc)
            ctx.mark(f"wsl_free{i}", svc)
        for s in range(NSTRIP):
            ctx.w(E, f"ld_x{s}")
            ops.bn_stats(bn12[:, 0:6], xbuf[s % 2][:, 0:512])
            inst = ops.bn_stats(bn12[:, 6:12], xbuf[s % 2][:, 512:1024])
            svc.inc(inst)
            ctx.wv(E, svc, svc.n)
            inst = ops.bn_aggr(bn2[:], bn12[:])
            svc.inc(inst)
            ctx.mark(f"bn{s}", svc)
            ctx.w(E, f"sqrt{s}")
            inst = ops.reciprocal(rstd[:], stdv[:])
            svc.inc(inst)
            ctx.wv(E, svc, svc.n)
            if s >= 2:
                ctx.w(E, f"xtev{(s - 2) * 8 + 7}")
            inst = ops.tensor_scalar(xln[s % 2][:], xbuf[s % 2][:],
                                     bn2[:, 0:1], rstd[:],
                                     ALU.subtract, ALU.mult)
            svc.inc(inst)
            ctx.mark(f"ln_apply{s}", svc)
        if KPH < 2:
            return
        for ip in range(NPIECE):
            g = gcol(ip)
            m = (g % T) // 512
            a, w = m % 4, m // 4
            Ct = ctab[32 * a:32 * a + 32, 512 * w:512 * w + 512]
            St = stab[32 * a:32 * a + 32, 512 * w:512 * w + 512]
            for cc in range(2):
                ctx.w(E, f"qk{ip}_{cc}")
                if cc == 0 and ip >= 2:
                    ctx.w(E, f"qdma{ip - 2}")
                dstq = qTb[ip % 2][:] if cc == 0 else kT[:, g:g + 512]
                src = pqk[cc]
                for h in range(2):
                    he = src[64 * h:64 * h + 32, :]
                    ho = src[64 * h + 32:64 * h + 64, :]
                    ops.tensor_tensor(rop[0][:], he, Ct, ALU.mult)
                    ops.tensor_tensor(rop[1][:], ho, St, ALU.mult)
                    ops.tensor_tensor(dstq[64 * h:64 * h + 32, :],
                                      rop[0][:], rop[1][:], ALU.subtract)
                    ops.tensor_tensor(rop[0][:], he, St, ALU.mult)
                    ops.tensor_tensor(rop[1][:], ho, Ct, ALU.mult)
                    inst = ops.tensor_tensor(dstq[64 * h + 32:64 * h + 64, :],
                                             rop[0][:], rop[1][:], ALU.add)
                svc.inc(inst)
                ctx.mark(f"rope{ip}_{cc}", svc)
                if ip + cc > 0:
                    pip, pcc = (ip, 0) if cc == 1 else (ip - 1, 1)
                    ctx.w(E, f"nmm{pip}_{pcc}")
                inst = ops.tensor_tensor(sqb[:], dstq[0:128, 0:512],
                                         dstq[0:128, 0:512], ALU.mult)
                svc.inc(inst)
                ctx.mark(f"sq{ip}_{cc}", svc)
            for cc in range(2):
                ctx.w(E, f"nmm{ip}_{cc}")
                dst = nmq if cc == 0 else nmk
                inst = ops.tensor_reduce(dst[:, ip:ip + 1], pn[0:2, :],
                                         AX.X, ALU.max)
                svc.inc(inst)
                ctx.mark(f"nred{ip}_{cc}", svc)
        ctx.wv(E, svc, svc.n)
        for cc, srcb in [(0, nmq), (1, nmk)]:
            for b in range(2):
                inst = ops.tensor_reduce(
                    nred[:, 2 * cc + b:2 * cc + b + 1, None],
                    srcb[:, :].rearrange("q (r p) -> q r p", p=4)
                    [:, :, 2 * b:2 * b + 2],
                    AX.XY, ALU.max)
                svc.inc(inst)
        ctx.mark("nred_done", svc)
        ctx.w(E, "nm_sqrt")
        inst = ops.tensor_tensor(small[0:1, 8:12], small[0:1, 0:4],
                                 small[0:1, 4:8], ALU.mult)
        svc.inc(inst)
        ctx.wv(E, svc, svc.n)
        inst = ops.tensor_scalar(small[0:1, 12:16], small[0:1, 8:12],
                                 -0.125, None, ALU.mult)
        svc.inc(inst)
        ctx.wv(E, svc, svc.n)
        inst = ops.tensor_copy(smbf[0:1, 0:4], small[0:1, 12:16])
        svc.inc(inst)
        ctx.mark("negm_bf", svc)

    def f_vector2(ctx, E, ops, C, P2, B2):
        svc = C["svc"]
        avp = P2["avp"]
        gun = 0
        for bsi in range(NBS):
            b, s = bsi // 16, bsi % 16
            nk = 4 * (s + 1)
            gun0 = gun
            gun += nk
            if bsi == 0:
                ctx.w(E, "consts")
            for kc in range(nk - 4, nk):
                blk = kc // 8
                j = kc % 8
                dd = kc - 4 * s
                ua = gun0 + (kc // 2) * 2
                ctx.w(E, f"expu{ua}")
                ctx.w(E, f"expu{ua + 1}")
                for h in range(2):
                    inst = ops.tensor_tensor(B2["pT"][h][blk % 2][:, j, :],
                                             B2["pT"][h][blk % 2][:, j, :],
                                             mask[:, dd, :], ALU.mult)
                svc.inc(inst)
                ctx.mark(f"mask{bsi}_{kc}", svc)
            ctx.w(E, f"avb{bsi}_{(nk - 1) // 8}")
            ops.reciprocal(B2["zif"][0:1, 0:512], avp[0][bsi % 2][64:65, :])
            inst = ops.reciprocal(B2["zif"][0:1, 512:1024],
                                  avp[1][bsi % 2][64:65, :])
            svc.inc(inst)
            ctx.wv(E, svc, svc.n)
            inst = ops.tensor_copy(B2["zib"][:], B2["zif"][:])
            svc.inc(inst)
            ctx.mark(f"zi{bsi}", svc)
            ctx.wv(E, C["sg_z"], 32 * (bsi + 1))
            if bsi >= 2:
                ctx.w(E, f"attdma{bsi - 2}")
            ops.tensor_tensor(B2["attb"][bsi % 2][:, 0:512],
                              avp[0][bsi % 2][0:64, :],
                              B2["zbs"][:, 0:512], ALU.mult)
            inst = ops.tensor_tensor(B2["attb"][bsi % 2][:, 512:1024],
                                     avp[1][bsi % 2][0:64, :],
                                     B2["zbs"][:, 512:1024], ALU.mult)
            svc.inc(inst)
            ctx.mark(f"attmul{bsi}", svc)
        ctx.mark("ph2_sv", svc)

    def f_vector3(ctx, E, ops, C, bufs, P3):
        svc = C["svc"]
        po, gup = P3["po"], P3["gup"]
        x1, xbuf3, xln2 = bufs["x1"], bufs["xbuf3"], bufs["xln2"]
        xq, gs = bufs["xq"], bufs["gs"]
        h_sb, hq_sb = bufs["h_sb"], bufs["hq_sb"]
        SCR3 = bufs["SCR3"]
        for it in range(NS3 + 1):
            prev, cur = it - 1, it
            if prev >= 0:
                ctx.w(E, f"down{prev}")
                for half in range(2):
                    inst = ops.tensor_tensor(
                        xbuf3[:, 512 * half:512 * (half + 1)],
                        P3["dn"][half][:],
                        x1[:, 512 * half:512 * (half + 1)], ALU.add)
                svc.inc(inst)
                ctx.mark(f"outadd{prev}", svc)
            if cur < NS3:
                ctx.w(E, f"po{cur}")
                ctx.w(E, f"ld_x3_{cur}")
                for half in range(2):
                    inst = ops.tensor_tensor(x1[:, 512 * half:512 * (half + 1)],
                                             po[half][:],
                                             xbuf3[:, 512 * half:512 * (half + 1)],
                                             ALU.add)
                svc.inc(inst)
                ctx.mark(f"x1_{cur}", svc)
                ops.bn_stats(bufs["bn12b"][:, 0:6], x1[:, 0:512])
                inst = ops.bn_stats(bufs["bn12b"][:, 6:12], x1[:, 512:1024])
                svc.inc(inst)
                ctx.wv(E, svc, svc.n)
                inst = ops.bn_aggr(bufs["bn2b"][:], bufs["bn12b"][:])
                svc.inc(inst)
                ctx.mark(f"bn3_{cur}", svc)
                ctx.w(E, f"sqrt3_{cur}")
                inst = ops.reciprocal(bufs["rstd3"][:], bufs["stdv3"][:])
                svc.inc(inst)
                ctx.wv(E, svc, svc.n)
                inst = ops.tensor_scalar(xln2[0:128, 0:D], x1[:],
                                         bufs["bn2b"][:, 0:1],
                                         bufs["rstd3"][:], ALU.subtract,
                                         ALU.mult)
                svc.inc(inst)
                ctx.wv(E, svc, svc.n)
                if cur >= 1:
                    ctx.w(E, f"xqtrg{cur - 1}_1")
                inst = emit_quant(ctx, ops, E, svc, xln2[0:128, 0:D], xq[:],
                                  128, D, SCR3)
                svc.inc(inst)
                ctx.mark(f"xq{cur}", svc)
                for ig in range(NIG):
                    ctx.w(E, f"silu{cur}_{ig}")
                    inst = ops.tensor_tensor(h_sb[:, IG * ig:IG * (ig + 1)],
                                             gup[ig % 2][:, 256:512],
                                             gs[ig % 2][:], ALU.mult)
                    svc.inc(inst)
                    ctx.mark(f"hm{cur}_{ig}", svc)
                if cur >= 2:
                    ctx.w(E, f"hqtrg{cur - 2}_5")
                for hh in range(2):
                    inst = emit_quant(ctx, ops, E, svc,
                                      h_sb[:, 1408 * hh:1408 * (hh + 1)],
                                      hq_sb[cur % 2][:, 1408 * hh:1408 * (hh + 1)],
                                      128, 1408, SCR3)
                    svc.inc(inst)
                ctx.mark(f"hq{cur}", svc)
        ctx.mark("ph3_sv", svc)

    def fix_ev(EV):
        EV["wsl_freeB0"] = ("svc", 0)
        for i in range(1, len(WTILES)):
            EV[f"wsl_freeB{i}"] = EV[f"wsl_free{i - 1}"]
        if KPH < 2:
            return
        if KPH < 3:
            return

    def mk_ctrs():
        return {n: Ctr(n) for n in semnames}

    mock_ops = MockOps()

    class _MockEng:
        def wait_ge(self, *a):
            pass

        def dma_start(self, **kw):
            return MockInst()

        def collective_compute(self, *a, **kw):
            return MockInst()

    MT = _MT()
    P2r = {"sps": MT, "avp": MT}
    P3r = {"po": MT, "dn": MT, "tr3": MT, "gup": MT}
    bufsr = {k: MT for k in
             ["att_t", "xbuf3", "x1", "xln2", "xq", "xqT", "gs", "h_sb",
              "hq_sb", "hqT", "wg_sb", "wu_sb", "wd_sb", "bn12b",
              "bn2b", "stdv3", "rstd3"]}
    bufsr["SCR3"] = {k: MT for k in ["xn", "sc", "iv", "c1", "c2", "c3", "c4"]}

    def f_tail(ctx, E, C, dummy):
        sg = C["sg"]
        ctx.w(E, "tail_ready")
        for so in range(NSTRIP):
            sg.inc(E.dma_start(out=out_h.ap()[128 * so:128 * (so + 1), :],
                               in_=dummy[:]), 16)
        ctx.wv(E, sg, sg.n)

    def f_tailv(ctx, ops, C):
        svc = C["svc"]
        svc.inc(ops.memset(xbuf[0][:], 0.0))
        ctx.mark("tail_ready", svc)

    B2r = {k: MT for k in ["qstrip", "pT", "zif", "zib", "zbs", "attb"]}
    ctxr = Ctx(True, EV)
    ctxr.C = mk_ctrs()
    mke = _MockEng()
    f_sync(ctxr, mke, ctxr.C)
    if KPH >= 2:
        f_sync2(ctxr, mke, ctxr.C, B2r)
    if KPH >= 3:
        f_sync3(ctxr, mke, ctxr.C, bufsr)
    f_gpsimd(ctxr, mke, ctxr.C)
    if KPH >= 2:
        f_gpsimd2(ctxr, mke, ctxr.C, B2r)
    if KPH >= 3:
        f_gpsimd3(ctxr, mke, ctxr.C, bufsr)
    f_tensor(ctxr, mke, mock_ops, ctxr.C)
    if KPH >= 2:
        f_tensor2(ctxr, mke, mock_ops, ctxr.C, P2r, B2r)
    if KPH >= 3:
        f_tensor3(ctxr, mke, mock_ops, ctxr.C, bufsr, P3r)
    f_scalar(ctxr, mke, mock_ops, ctxr.C)
    if KPH >= 2:
        f_scalar2(ctxr, mke, mock_ops, ctxr.C, P2r, B2r)
    if KPH >= 3:
        f_scalar3(ctxr, mke, mock_ops, ctxr.C, bufsr, P3r)
    f_vector(ctxr, mke, mock_ops, ctxr.C)
    if KPH >= 2:
        f_vector2(ctxr, mke, mock_ops, ctxr.C, P2r, B2r)
    if KPH >= 3:
        f_vector3(ctxr, mke, mock_ops, ctxr.C, bufsr, P3r)
    if KPH < 3:
        f_tailv(ctxr, mock_ops, ctxr.C)
        f_tail(ctxr, mke, ctxr.C, MT)
    fix_ev(EV)
    totals = {n: c.n for n, c in ctxr.C.items()}

    ctx = Ctx(False, EV)
    ctx.C = mk_ctrs()
    for n in semnames:
        ctx.C[n].sem = handles[n]

    with nc.Block() as block:
        @block.sync
        def _(sync):
            f_sync(ctx, sync, ctx.C)

        @block.gpsimd
        def _(gp):
            f_gpsimd(ctx, gp, ctx.C)

        @block.tensor
        def _(tn):
            f_tensor(ctx, tn, nc.tensor, ctx.C)

        @block.scalar
        def _(sc_):
            f_scalar(ctx, sc_, nc.scalar, ctx.C)

        @block.vector
        def _(ve):
            f_vector(ctx, ve, nc.vector, ctx.C)

        if KPH < 2:
            @block.vector
            def _(ve):
                f_tailv(ctx, nc.vector, ctx.C)

            @block.gpsimd
            def _(gp):
                f_tail(ctx, gp, ctx.C, xbuf[0])
        es_ps1.close()
        es0.close()
        if KPH < 2:
            for n in semnames:
                assert ctx.C[n].n == totals[n], (n, ctx.C[n].n, totals[n])
            es12.close()
            es3.close()
            esg.close()
            return nc
        B2 = {
            "qstrip": [sb(es12, f"qstrip{i}", [128, 512], BF16)
                       for i in range(2)],
            "pT": [[sb(es12, f"pT{h}_{i}", [128, 8, 512], BF16)
                    for i in range(2)] for h in range(2)],
            "zif": sb(es12, "zif", [1, 1024], F32),
            "zib": sb(es12, "zib", [1, 1024], BF16),
            "zbs": sb(es12, "zbs", [64, 1024], BF16),
            "attb": [sb(es12, f"attb{i}", [64, 1024], BF16)
                     for i in range(2)],
        }
        P2 = {
            "sps": [pst(es_ps2, f"sps{i}", [128, 1024]) for i in range(2)],
            "avp": [[pst(es_ps2, f"avp{h}_{i}", [65, 512]) for i in range(2)]
                    for h in range(2)],
        }

        @block.sync
        def _(sync):
            f_sync2(ctx, sync, ctx.C, B2)

        @block.gpsimd
        def _(gp):
            f_gpsimd2(ctx, gp, ctx.C, B2)

        @block.tensor
        def _(tn):
            f_tensor2(ctx, tn, nc.tensor, ctx.C, P2, B2)

        @block.scalar
        def _(sc_):
            f_scalar2(ctx, sc_, nc.scalar, ctx.C, P2, B2)

        @block.vector
        def _(ve):
            f_vector2(ctx, ve, nc.vector, ctx.C, P2, B2)

        es_ps2.close()
        if KPH < 3:
            @block.vector
            def _(ve):
                f_tailv(ctx, nc.vector, ctx.C)

            @block.gpsimd
            def _(gp):
                f_tail(ctx, gp, ctx.C, xbuf[0])
            for n in semnames:
                assert ctx.C[n].n == totals[n], (n, ctx.C[n].n, totals[n])
            es12.close()
            es3.close()
            esg.close()
            return nc
        es12.close()
        bufs = {}
        bufs["wg_sb"] = sb(es3, "wg_sb", [128, 8, II], BF16)
        bufs["wu_sb"] = sb(es3, "wu_sb", [128, 8, II], BF16)
        bufs["wd_sb"] = sb(es3, "wd_sb", [128, 22, D], BF16)
        bufs["att_t"] = sb(es3, "att_t", [128, 8, 128], BF16)
        bufs["xbuf3"] = sb(es3, "xbuf3", [128, D], F32)
        bufs["x1"] = sb(es3, "x1", [128, D], F32)
        bufs["xq"] = sb(es3, "xq", [128, D], BF16)
        bufs["xqT"] = sb(es3, "xqT", [128, 8, 128], BF16)
        bufs["gs"] = [sb(es3, f"gs{i}", [128, IG], BF16) for i in range(2)]
        bufs["h_sb"] = sb(es3, "h_sb", [128, II], BF16)
        bufs["hq_sb"] = [sb(es3, f"hq_sb{i}", [128, II], BF16) for i in range(2)]
        bufs["hqT"] = sb(es3, "hqT", [128, 22, 128], BF16)
        bufs["bn12b"] = sb(es3, "bn12b", [128, 12], F32)
        bufs["bn2b"] = sb(es3, "bn2b", [128, 2], F32)
        bufs["stdv3"] = sb(es3, "stdv3", [128, 1], F32)
        bufs["rstd3"] = sb(es3, "rstd3", [128, 1], F32)
        QW3 = 1408
        q3 = {"xn": sb(es3, "q3xn", [128, QW3], F32),
              "sc": sb(es3, "q3sc", [128, QW3 // 16], F32),
              "iv": sb(es3, "q3iv", [128, QW3 // 16], F32)}
        for k in ["c1", "c2", "c3", "c4"]:
            q3[k] = sb(es3, f"q3{k}", [128, QW3], BF16)
        bufs["SCR3"] = q3
        bufs["xln2"] = q3["xn"]
        P3 = {
            "po": [pst(es3, f"po{i}", [128, 512]) for i in range(2)],
            "dn": [pst(es3, f"dn{i}", [128, 512]) for i in range(2)],
            "tr3": [pst(es3, f"tr3{i}", [128, 4, 128], BF16) for i in range(2)],
            "gup": [pst(es3, f"gup{i}", [128, 512]) for i in range(2)],
        }

        @block.sync
        def _(sync):
            f_sync3(ctx, sync, ctx.C, bufs)

        @block.gpsimd
        def _(gp):
            f_gpsimd3(ctx, gp, ctx.C, bufs)

        @block.tensor
        def _(tn):
            f_tensor3(ctx, tn, nc.tensor, ctx.C, bufs, P3)

        @block.scalar
        def _(sc_):
            f_scalar3(ctx, sc_, nc.scalar, ctx.C, bufs, P3)

        @block.vector
        def _(ve):
            f_vector3(ctx, ve, nc.vector, ctx.C, bufs, P3)

    for n in semnames:
        assert ctx.C[n].n == totals[n], (n, ctx.C[n].n, totals[n])
    es3.close()
    esg.close()
    return nc


_NC_CACHE = {}


def get_nc(debug=False):
    key = bool(debug)
    if key not in _NC_CACHE:
        _NC_CACHE[key] = build_nc(key)
    return _NC_CACHE[key]


def make_inputs(x, qkv_w, qkv_b, out_w, out_b, wg, wu, wd):
    inv_freq = (1.0 / 10000.0 **
                (np.arange(0, HD, 2, dtype=np.float32) / np.float32(HD))
                ).astype(np.float32)
    t = np.arange(T, dtype=np.float32)
    freqs = (t[:, None] * inv_freq[None, :]).astype(np.float32)
    rope = np.concatenate([np.sin(freqs.astype(np.float64)),
                           np.cos(freqs.astype(np.float64))], -1
                          ).astype(np.float32)
    def pack_tab(tab):
        return np.ascontiguousarray(
            tab.reshape(32, 4, 4, 512).transpose(2, 0, 1, 3).reshape(128, 2048))
    S_tab = pack_tab(np.ascontiguousarray(rope[:, 0::2].T))
    C_tab = pack_tab(np.ascontiguousarray(rope[:, 1::2].T))
    kk = np.arange(128)[:, None]
    qq = np.arange(512)[None, :]
    mask01 = np.ascontiguousarray(
        np.stack([(qq >= kk + 128 * dd).astype(np.float32)
                  for dd in range(4)], axis=1))
    ident = np.eye(128, dtype=np.float32)
    e2m = np.zeros((128, 2), np.float32)
    e2m[0:64, 0] = 1.0
    e2m[64:128, 1] = 1.0
    onesm = np.ones((1, 512), np.float32)

    in_maps = []
    for c in range(NC):
        hA, hB = 2 * c, 2 * c + 1
        perm = []
        for sec in range(2):
            for hh in (hA, hB):
                base = sec * D + hh * HD
                perm += [base + j for j in range(0, HD, 2)]
                perm += [base + j for j in range(1, HD, 2)]
        vcols = [2 * D + hh * HD + j for hh in (hA, hB) for j in range(HD)]
        xo = np.ascontiguousarray(
            x[:, 1024 * c:1024 * (c + 1), :].reshape(TOK, D))
        m = {
            "x_own": xo,
            "wqk": np.ascontiguousarray(qkv_w[:, perm]),
            "wv": np.ascontiguousarray(qkv_w[:, vcols]),
            "bqk": np.ascontiguousarray(qkv_b[perm])[None, :],
            "bv": np.ascontiguousarray(qkv_b[vcols])[None, :],
            "wout": np.ascontiguousarray(out_w),
            "bout": np.ascontiguousarray(out_b)[None, :],
            "wg_sl": np.ascontiguousarray(wg[WGR * c:WGR * (c + 1), :]),
            "wu_sl": np.ascontiguousarray(wu[WGR * c:WGR * (c + 1), :]),
            "wd_sl": np.ascontiguousarray(wd[WDR * c:WDR * (c + 1), :]),
            "ctab": C_tab, "stab": S_tab, "mask01": mask01,
            "ident": ident, "e2": e2m, "ones_in": onesm,
        }
        in_maps.append(m)
    return in_maps


def kernel(L0_attn_x, L0_attn_qkv_w, L0_attn_qkv_b, L0_attn_out_w,
           L0_attn_out_b, L0_ffn_w_gate, L0_ffn_w_up, L0_ffn_w_down):
    debug = os.environ.get("KDBG", "") == "1"
    in_maps = make_inputs(np.asarray(L0_attn_x), np.asarray(L0_attn_qkv_w),
                          np.asarray(L0_attn_qkv_b), np.asarray(L0_attn_out_w),
                          np.asarray(L0_attn_out_b), np.asarray(L0_ffn_w_gate),
                          np.asarray(L0_ffn_w_up), np.asarray(L0_ffn_w_down))
    nc = get_nc(debug)
    res = run_bass_kernel_spmd(nc, in_maps, core_ids=list(range(NC)))
    out = np.empty((B, T, D), dtype=np.float32)
    for c in range(NC):
        r = np.asarray(res.results[c]["out"], dtype=np.float32)
        out[0, 1024 * c:1024 * (c + 1), :] = r[0:1024]
        out[1, 1024 * c:1024 * (c + 1), :] = r[1024:2048]
    if debug:
        kernel.dbg = [dict(res.results[c]) for c in range(NC)]
    return out


# version 05
# speedup vs baseline: 1.1950x; 1.1950x over previous
import contextlib
import os
import numpy as np
import ml_dtypes
import concourse.bass as bass
import concourse.mybir as mybir
from concourse.bass_utils import run_bass_kernel_spmd
from concourse.aot_env import allow_ici_all_to_all

os.environ.setdefault("TRN_ALLOW_ICI_ALL_TO_ALL", "1")

F32 = mybir.dt.float32
BF16 = mybir.dt.bfloat16
U32 = mybir.dt.uint32
AF = mybir.ActivationFunctionType
ALU = mybir.AluOpType
AX = mybir.AxisListType

B, T, D, H, HD, II = 2, 8192, 1024, 16, 64, 2816
NC = 8
TOK = B * T // NC
NSTRIP = TOK // 128
WGR = II // NC
WDR = D // NC
NW = D * WGR * 2 + II * WDR
OFF_WU = D * WGR
OFF_WD = 2 * D * WGR
EPS = 1e-5
MIDBITS = [0x3E800000, 0x3F400000, 0x3FA00000, 0x3FE00000,
           0x40200000, 0x40600000, 0x40A00000]
IG = 256
NIG = II // IG
KPH = int(os.environ.get("KPHASE", "3"))
NS3 = int(os.environ.get("KSTRIP", "16"))
NPIECE = 32
NBS = 2 * 16
WTILES = [(0, 0, 128), (0, 1, 128), (0, 2, 96),
          (1, 0, 128), (1, 1, 128), (1, 2, 96),
          (2, 0, 128)]


def gcol(ip):
    r, p = ip // 4, ip % 4
    return (p // 2) * T + 1024 * r + (p % 2) * 512


class Ctr:
    def __init__(self, name):
        self.name, self.sem, self.n = name, None, 0

    def inc(self, inst, k=1):
        if inst is not None and self.sem is not None:
            inst.then_inc(self.sem, k)
        self.n += k
        return self.n


class MockInst:
    def then_inc(self, sem, k=None):
        return self


class MockOps:
    def __getattr__(self, name):
        return lambda *a, **kw: MockInst()


class _MT:
    def __getitem__(self, i):
        return self

    def __call__(self, *a, **kw):
        return self

    def __getattr__(self, name):
        return self


class Ctx:
    def __init__(self, record, EV):
        self.record = record
        self.EV = EV
        self.C = None

    def mark(self, key, ctr):
        if self.record:
            self.EV[key] = (ctr.name, ctr.n)
        else:
            assert self.EV.get(key) == (ctr.name, ctr.n), (
                "EV mismatch", key, self.EV.get(key), ctr.name, ctr.n)

    def w(self, eng, key):
        if not self.record:
            name, val = self.EV[key]
            if val > 0:
                eng.wait_ge(self.C[name].sem, val)

    def wv(self, eng, ctr, val):
        if not self.record and val > 0:
            eng.wait_ge(ctr.sem, val)


def emit_rsqrt(v, dst, var_ap, t1, t2):
    P = var_ap.shape[0]
    v.tensor_scalar(t1[0:P, :], var_ap, EPS, None, ALU.add)
    v.tensor_scalar(t2[0:P, :].bitcast(U32), t1[0:P, :].bitcast(U32),
                    -0.5, float(0x5F3759DF), ALU.mult, ALU.add)
    y = t2[0:P, :]
    for _ in range(3):
        v.tensor_tensor(dst[0:P, :], y, y, ALU.mult)
        v.tensor_tensor(dst[0:P, :], dst[0:P, :], t1[0:P, :], ALU.mult)
        v.tensor_scalar(dst[0:P, :], dst[0:P, :], -0.5, 1.5, ALU.mult, ALU.add)
        v.tensor_tensor(t2[0:P, :], y, dst[0:P, :], ALU.mult)
    return v.tensor_copy(dst[0:P, :], t2[0:P, :])


def emit_quant(ctx, v, eng, svc, src_ap, dst_ap, P, W, scr):
    nb = W // 16
    xn, sc, iv = scr["xn"], scr["sc"], scr["iv"]
    c1, c2, c3, c4 = scr["c1"], scr["c2"], scr["c3"], scr["c4"]
    svc.inc(v.tensor_reduce(sc[0:P, 0:nb, None],
                            src_ap.rearrange("p (b k) -> p b k", k=16),
                            AX.X, ALU.max, apply_absolute_value=True))
    ctx.wv(eng, svc, svc.n)
    svc.inc(v.tensor_scalar(sc[0:P, 0:nb], sc[0:P, 0:nb], 1.0 / 6.0, 1e-12,
                            ALU.mult, ALU.max))
    ctx.wv(eng, svc, svc.n)
    svc.inc(v.reciprocal(iv[0:P, 0:nb], sc[0:P, 0:nb]))
    ctx.wv(eng, svc, svc.n)
    v.tensor_tensor(xn[0:P, 0:W].rearrange("p (b k) -> p b k", k=16),
                    src_ap.rearrange("p (b k) -> p b k", k=16),
                    iv[0:P, 0:nb, None].broadcast_to([P, nb, 16]), ALU.mult)
    xn_u = xn[0:P, 0:W].bitcast(U32)
    v.tensor_scalar(c4[0:P, 0:W], xn[0:P, 0:W], 0.0, None, ALU.is_lt)
    v.tensor_scalar(c4[0:P, 0:W], c4[0:P, 0:W], -2.0, 1.0, ALU.mult, ALU.add)
    v.tensor_scalar(xn_u, xn_u, 0x7FFFFFFF, None, ALU.bitwise_and)
    v.tensor_scalar(c1[0:P, 0:W], xn_u, MIDBITS[0], None, ALU.is_gt)
    v.tensor_scalar(c2[0:P, 0:W], xn_u, MIDBITS[1], None, ALU.is_gt)
    v.tensor_tensor(c1[0:P, 0:W], c1[0:P, 0:W], c2[0:P, 0:W], ALU.add)
    v.tensor_scalar(c2[0:P, 0:W], xn_u, MIDBITS[2], None, ALU.is_gt)
    v.tensor_scalar(c3[0:P, 0:W], xn_u, MIDBITS[3], None, ALU.is_gt)
    v.tensor_tensor(c2[0:P, 0:W], c2[0:P, 0:W], c3[0:P, 0:W], ALU.add)
    v.tensor_tensor(c1[0:P, 0:W], c1[0:P, 0:W], c2[0:P, 0:W], ALU.add)
    v.tensor_scalar(c1[0:P, 0:W], c1[0:P, 0:W], 0.5, None, ALU.mult)
    v.tensor_scalar(c2[0:P, 0:W], xn_u, MIDBITS[4], None, ALU.is_gt)
    v.tensor_scalar(c3[0:P, 0:W], xn_u, MIDBITS[5], None, ALU.is_gt)
    v.tensor_tensor(c2[0:P, 0:W], c2[0:P, 0:W], c3[0:P, 0:W], ALU.add)
    v.tensor_tensor(c1[0:P, 0:W], c1[0:P, 0:W], c2[0:P, 0:W], ALU.add)
    v.tensor_scalar(c2[0:P, 0:W], xn_u, MIDBITS[6], None, ALU.is_gt)
    v.tensor_scalar(c2[0:P, 0:W], c2[0:P, 0:W], 2.0, None, ALU.mult)
    v.tensor_tensor(c1[0:P, 0:W], c1[0:P, 0:W], c2[0:P, 0:W], ALU.add)
    v.tensor_tensor(c1[0:P, 0:W], c1[0:P, 0:W], c4[0:P, 0:W], ALU.mult)
    return v.tensor_tensor(dst_ap.rearrange("p (b k) -> p b k", k=16),
                           c1[0:P, 0:W].rearrange("p (b k) -> p b k", k=16),
                           sc[0:P, 0:nb, None].broadcast_to([P, nb, 16]),
                           ALU.mult)


def build_nc(debug=False):
    nc = bass.Bass("TRN2", target_bir_lowering=False)

    x_h = nc.dram_tensor("x_own", [TOK, D], F32, kind="ExternalInput")
    wqk_h = nc.dram_tensor("wqk", [D, 256], F32, kind="ExternalInput")
    wv_h = nc.dram_tensor("wv", [D, 128], F32, kind="ExternalInput")
    bqk_h = nc.dram_tensor("bqk", [1, 256], F32, kind="ExternalInput")
    bv_h = nc.dram_tensor("bv", [1, 128], F32, kind="ExternalInput")
    wout_h = nc.dram_tensor("wout", [D, D], F32, kind="ExternalInput")
    bout_h = nc.dram_tensor("bout", [1, D], F32, kind="ExternalInput")
    wg_h = nc.dram_tensor("wg_sl", [WGR, D], F32, kind="ExternalInput")
    wu_h = nc.dram_tensor("wu_sl", [WGR, D], F32, kind="ExternalInput")
    wd_h = nc.dram_tensor("wd_sl", [WDR, II], F32, kind="ExternalInput")
    ctab_h = nc.dram_tensor("ctab", [128, 2048], F32, kind="ExternalInput")
    stab_h = nc.dram_tensor("stab", [128, 2048], F32, kind="ExternalInput")
    mask_h = nc.dram_tensor("mask01", [128, 4, 512], F32, kind="ExternalInput")
    id_h = nc.dram_tensor("ident", [128, 128], F32, kind="ExternalInput")
    e2_h = nc.dram_tensor("e2", [128, 2], F32, kind="ExternalInput")
    one_h = nc.dram_tensor("ones_in", [1, 512], F32, kind="ExternalInput")
    out_h = nc.dram_tensor("out", [TOK, D], F32, kind="ExternalOutput")

    xlnT_in = nc.dram_tensor("xlnT_in", [D, TOK], BF16)
    xlnT_all = nc.dram_tensor("xlnT_all", [NC * D, TOK], BF16)
    qT_dram = nc.dram_tensor("qT_dram", [128, B * T], BF16)
    attT_in = nc.dram_tensor("attT_in", [NC * 128, TOK], BF16)
    attT_out = nc.dram_tensor("attT_out", [NC * 128, TOK], BF16)
    ccw_in = nc.dram_tensor("ccw_in", [NW], BF16)
    ccw_all = nc.dram_tensor("ccw_all", [NC, NW], BF16)
    nm_dram = nc.dram_tensor("nm_dram", [8], F32)
    zscr = nc.dram_tensor("zscr", [1024], BF16)

    dbg = {}
    if debug:
        dbg["xlnT"] = nc.dram_tensor("d_xlnT", [D, TOK], BF16, kind="ExternalOutput")
        dbg["qT"] = nc.dram_tensor("d_qT", [128, B * T], BF16, kind="ExternalOutput")
        dbg["att"] = nc.dram_tensor("d_att", [NC * 128, TOK], BF16, kind="ExternalOutput")
        dbg["negm"] = nc.dram_tensor("d_negm", [128, 4], F32, kind="ExternalOutput")
        dbg["ccw"] = nc.dram_tensor("d_ccw", [NW], BF16, kind="ExternalOutput")

    esg = contextlib.ExitStack()
    es12 = contextlib.ExitStack()
    es0 = contextlib.ExitStack()
    es3 = contextlib.ExitStack()
    es_ps1 = contextlib.ExitStack()
    es_ps2 = contextlib.ExitStack()

    def sb(es, name, shape, dt):
        return es.enter_context(nc.sbuf_tensor(name, shape, dt))

    def pst(es, name, shape, dt=F32):
        return es.enter_context(nc.psum_tensor(name, shape, dt))

    ident = sb(esg, "ident_s", [128, 128], BF16)
    ones = sb(esg, "ones_s", [1, 512], BF16)
    e2 = sb(esg, "e2_s", [128, 2], BF16)
    wout = sb(esg, "wout_s", [128, 8, D], BF16)
    bout = sb(esg, "bout_s", [1, D], BF16)
    negm = sb(esg, "negm_s", [128, 4], F32)
    small = sb(esg, "small_s", [1, 16], F32)
    epsb = sb(esg, "epsb_s", [128, 1], F32)
    smbf = sb(esg, "smbf_s", [1, 8], BF16)

    semnames = ["sd", "sg", "scc", "stn", "ssc", "svc",
                "s_ws", "s_x0", "s_x1", "s_xt0", "s_xt1", "s_q0", "s_q1",
                "s_att", "s_x3", "sgw", "sg_c0", "sg_c1", "sg_q",
                "sg_a0", "sg_a1", "sg_z"]
    handles = {n: esg.enter_context(nc.semaphore(n)) for n in semnames}
    EV = {}

    wqk = sb(es12, "wqk_s", [128, 8, 256], BF16)
    wv = sb(es12, "wv_s", [128, 8, 128], BF16)
    bqk = sb(es12, "bqk_s", [1, 256], BF16)
    bv = sb(es12, "bv_s", [1, 128], BF16)
    mask = sb(es12, "mask_s", [128, 4, 512], BF16)
    ctab = sb(es12, "ctab_s", [128, 2048], F32)
    stab = sb(es12, "stab_s", [128, 2048], F32)
    xt = [sb(es12, f"xt{i}", [128, 8, 512], BF16) for i in range(2)]
    qTb = [sb(es12, f"qTb{i}", [128, 512], BF16) for i in range(2)]
    kT = sb(es12, "kT", [128, B * T], BF16)
    V = sb(es12, "V", [128, 128, 130], BF16)
    rop = [sb(es12, f"rop{i}", [32, 512], F32) for i in range(2)]
    sqb = sb(es12, "sqb", [128, 512], BF16)
    nmq = sb(es12, "nmq", [2, 32], F32)
    nmk = sb(es12, "nmk", [2, 32], F32)
    nred = sb(es12, "nred", [2, 4], F32)
    xbuf = [sb(es0, f"xbuf{i}", [128, D], F32) for i in range(2)]
    xln = [sb(es0, f"xln{i}", [128, D], BF16) for i in range(2)]
    bn12 = sb(es0, "bn12", [128, 12], F32)
    bn2 = sb(es0, "bn2", [128, 2], F32)
    stdv = sb(es0, "stdv", [128, 1], F32)
    rstd = sb(es0, "rstd", [128, 1], F32)
    q_rs1 = sb(es0, "q_rs1", [128, 1], F32)
    cstrip = [sb(es0, f"cstrip{i}", [128, 8, 128], BF16) for i in range(2)]
    QW = 1408
    SCR0 = {
        "xn": sb(es0, "q_xn", [128, QW], F32),
        "sc": sb(es0, "q_sc", [128, QW // 16], F32),
        "iv": sb(es0, "q_iv", [128, QW // 16], F32),
        "c1": sb(es0, "q_c1", [128, QW], BF16),
        "c2": sb(es0, "q_c2", [128, QW], BF16),
        "c3": sb(es0, "q_c3", [128, QW], BF16),
        "c4": sb(es0, "q_c4", [128, QW], BF16),
    }
    wslice = sb(es0, "wslice", [128, II], F32)
    wqrow = sb(es0, "wqrow", [128, II], BF16)
    wT_out = sb(es0, "wT_out", [128, 8, WGR], BF16)
    wdT_out = sb(es0, "wdT_out", [128, 22, 128], BF16)
    pqk = [pst(es_ps1, f"pqk{i}", [128, 512]) for i in range(2)]
    pv = [pst(es_ps1, f"pv{i}", [128, 128]) for i in range(2)]
    pn = pst(es_ps1, "pn", [2, 512])
    ptr = [pst(es_ps1, f"ptr{i}", [128, 128], BF16) for i in range(2)]


    def f_sync(ctx, E, C):
        def d(ctr, dst, src):
            return ctr.inc(E.dma_start(out=dst, in_=src), 16)

        d(C["s_ws"], ctab[:], ctab_h.ap())
        d(C["s_ws"], stab[:], stab_h.ap())
        for i, (wi, ti, P) in enumerate(WTILES):
            ctx.w(E, f"wsl_freeB{i}")
            wh = [wg_h, wu_h, wd_h][wi]
            WR = II if wi == 2 else D
            d(C["s_ws"], wslice[0:P, 0:WR], wh.ap()[128 * ti:128 * ti + P, :])
            ctx.mark(f"ld_w{i}", C["s_ws"])
        for s in range(NSTRIP):
            if s >= 2:
                ctx.w(E, f"ln_apply{s - 2}")
            d(C[f"s_x{s % 2}"], xbuf[s % 2][:], x_h.ap()[128 * s:128 * (s + 1), :])
            ctx.mark(f"ld_x{s}", C[f"s_x{s % 2}"])
        if KPH < 2:
            return
        for ip in range(NPIECE):
            r, p = ip // 4, ip % 4
            if ip == 0:
                ctx.wv(E, C["scc"], 2)
            if ip >= 2:
                ctx.w(E, f"xt_used{ip - 2}")
            d(C[f"s_xt{ip % 2}"], xt[ip % 2][:],
              xlnT_all.ap()[D * r:D * (r + 1), 512 * p:512 * (p + 1)]
              .rearrange("(c p) t -> p c t", p=128))
            ctx.mark(f"ld_xt{ip}", C[f"s_xt{ip % 2}"])
    def f_sync2(ctx, E, C, B2):
        def d(ctr, dst, src):
            return ctr.inc(E.dma_start(out=dst, in_=src), 16)

        for bsi in range(NBS):
            b, s = bsi // 16, bsi % 16
            if bsi == 0:
                ctx.wv(E, C["sg_q"], 16 * NPIECE)
            if bsi >= 2:
                ctx.w(E, f"s_done{bsi - 2}")
            d(C[f"s_q{bsi % 2}"], B2["qstrip"][bsi % 2][:],
              qT_dram.ap()[:, b * T + 512 * s: b * T + 512 * (s + 1)])
            ctx.mark(f"ld_q{bsi}", C[f"s_q{bsi % 2}"])

    def f_sync3(ctx, E, C, bufs):
        def d(ctr, dst, src):
            return ctr.inc(E.dma_start(out=dst, in_=src), 16)

        for so in range(NS3):
            if so == 0:
                ctx.wv(E, C["scc"], 3)
                ctx.w(E, "ph2_tn")
                ctx.w(E, "ph2_sv")
                ctx.w(E, "ph2_sc")
            if so >= 1:
                ctx.w(E, f"po{so - 1}")
            d(C["s_att"], bufs["att_t"][:],
              attT_out.ap().rearrange("(r p) t -> p r t", p=128)
              [:, :, 128 * so:128 * (so + 1)])
            ctx.mark(f"ld_att{so}", C["s_att"])
            if so >= 1:
                ctx.w(E, f"x1_{so - 1}")
            d(C["s_x3"], bufs["xbuf3"][:], x_h.ap()[128 * so:128 * (so + 1), :])
            ctx.mark(f"ld_x3_{so}", C["s_x3"])

    def f_gpsimd(ctx, E, C):
        sg, sgw, scc = C["sg"], C["sgw"], C["scc"]

        def d(ctr, dst, src):
            return ctr.inc(E.dma_start(out=dst, in_=src), 16)

        d(sgw, ident[:], id_h.ap())
        d(sgw, ones[:], one_h.ap())
        d(sgw, e2[:], e2_h.ap())
        d(sgw, mask[:], mask_h.ap())
        d(sgw, wqk[:], wqk_h.ap().rearrange("(c p) n -> p c n", p=128))
        d(sgw, wv[:], wv_h.ap().rearrange("(c p) n -> p c n", p=128))
        d(sgw, bqk[:], bqk_h.ap())
        d(sgw, bv[:], bv_h.ap())
        d(sgw, bout[:], bout_h.ap())
        d(sgw, wout[:], wout_h.ap().rearrange("(c p) n -> p c n", p=128))
        ctx.mark("w_bf16", sgw)
        ctx.mark("consts", sgw)
        ctx.w(E, "wT_wg_done")
        d(sg, ccw_in.ap()[0:OFF_WU].rearrange("(c p i) -> p c i", p=128, i=WGR),
          wT_out[:])
        ctx.mark("ccw_wg_dma", sg)
        ctx.w(E, "wT_wu_done")
        d(sg, ccw_in.ap()[OFF_WU:OFF_WD].rearrange("(c p i) -> p c i", p=128, i=WGR),
          wT_out[:])
        ctx.mark("ccw_wu_dma", sg)
        ctx.w(E, "wT_wd_done")
        d(sg, ccw_in.ap()[OFF_WD:NW].rearrange("(c p h) -> p c h", p=128, h=128),
          wdT_out[:])
        ctx.wv(E, sg, sg.n)
        scc.inc(E.collective_compute(
            "AllGather", ALU.bypass, replica_groups=[list(range(NC))],
            ins=[ccw_in.ap().opt()], outs=[ccw_all.ap().opt()]))
        for s in range(NSTRIP):
            ctx.w(E, f"xtev{s * 8 + 7}")
            d(C[f"sg_c{s % 2}"], xlnT_in.ap().rearrange("(c p) t -> p c t", p=128)
              [:, :, 128 * s:128 * (s + 1)], cstrip[s % 2][:])
            ctx.mark(f"cdma{s}", C[f"sg_c{s % 2}"])
        ctx.wv(E, C["sg_c0"], C["sg_c0"].n)
        ctx.wv(E, C["sg_c1"], C["sg_c1"].n)
        scc.inc(E.collective_compute(
            "AllGather", ALU.bypass, replica_groups=[list(range(NC))],
            ins=[xlnT_in.ap().opt()], outs=[xlnT_all.ap().opt()]))
        if KPH < 2:
            return
        for ip in range(NPIECE):
            ctx.w(E, f"rope{ip}_0")
            g = gcol(ip)
            d(C["sg_q"], qT_dram.ap()[:, g:g + 512], qTb[ip % 2][:])
            ctx.mark(f"qdma{ip}", C["sg_q"])
        ctx.w(E, "nred_done")
        sg.inc(E.dma_start(out=nm_dram.ap()[0:4].rearrange("(b h) -> h b", h=2),
                           in_=nred[0:2, 0:2],
                           allow_slow_non_contiguous=True), 16)
        sg.inc(E.dma_start(out=nm_dram.ap()[4:8].rearrange("(b h) -> h b", h=2),
                           in_=nred[0:2, 2:4],
                           allow_slow_non_contiguous=True), 16)
        ctx.wv(E, sg, sg.n)
        d(sg, small[0:1, 0:8], nm_dram.ap()[None, :])
        ctx.mark("nm_back", sg)
    def f_gpsimd2(ctx, E, C, B2):
        scc = C["scc"]

        def d(ctr, dst, src):
            return ctr.inc(E.dma_start(out=dst, in_=src), 16)

        sgz = C["sg_z"]
        for bsi in range(NBS):
            b, s = bsi // 16, bsi % 16
            o = s // 2
            oc = b * 1024 + (s % 2) * 512
            ctx.w(E, f"zi{bsi}")
            d(sgz, zscr.ap()[None, :], B2["zib"][:])
            ctx.wv(E, sgz, sgz.n)
            d(sgz, B2["zbs"][:],
              zscr.ap()[None, :].broadcast_to([64, 1024]))
            ctx.w(E, f"attmul{bsi}")
            d(C[f"sg_a{bsi % 2}"], attT_in.ap()[128 * o:128 * (o + 1), oc:oc + 512]
              .rearrange("(j p) t -> p j t", p=64),
              B2["attb"][bsi % 2][:].rearrange("p (j t) -> p j t", j=2))
            ctx.mark(f"attdma{bsi}", C[f"sg_a{bsi % 2}"])
        ctx.wv(E, C["sg_a0"], C["sg_a0"].n)
        ctx.wv(E, C["sg_a1"], C["sg_a1"].n)
        with allow_ici_all_to_all():
            scc.inc(E.collective_compute(
                "AllToAll", ALU.bypass, replica_groups=[list(range(NC))],
                ins=[attT_in.ap().opt()], outs=[attT_out.ap().opt()]))

    def f_gpsimd3(ctx, E, C, bufs):
        sg = C["sg"]

        def d(dst, src):
            return sg.inc(E.dma_start(out=dst, in_=src), 16)

        ctx.wv(E, C["sg_a0"], C["sg_a0"].n)
        ctx.wv(E, C["sg_a1"], C["sg_a1"].n)
        ctx.wv(E, C["scc"], 3)
        ctx.w(E, "ph2_tn")
        ctx.w(E, "ph2_sv")
        ctx.w(E, "ph2_sc")
        for c in range(8):
            d(bufs["wg_sb"][:, c, :].rearrange("p (r i) -> p r i", i=WGR),
              ccw_all.ap()[:, 128 * c * WGR:(128 * c + 128) * WGR]
              .rearrange("r (p i) -> p r i", p=128))
            d(bufs["wu_sb"][:, c, :].rearrange("p (r i) -> p r i", i=WGR),
              ccw_all.ap()[:, OFF_WU + 128 * c * WGR:
                           OFF_WU + (128 * c + 128) * WGR]
              .rearrange("r (p i) -> p r i", p=128))
        for ic in range(22):
            d(bufs["wd_sb"][:, ic, :].rearrange("p (r h) -> p r h", h=128),
              ccw_all.ap()[:, OFF_WD + 128 * ic * 128:
                           OFF_WD + (128 * ic + 128) * 128]
              .rearrange("r (p h) -> p r h", p=128))
        ctx.mark("ffnw", sg)
        for so in range(NSTRIP):
            if so < NS3:
                ctx.w(E, f"outadd{so}")
            d(out_h.ap()[128 * so:128 * (so + 1), :],
              bufs["out_sb"][:, 0:2048].bitcast(F32))
            if so < NS3:
                ctx.mark(f"outdma{so}", sg)
        if debug:
            d(dbg["xlnT"].ap(), xlnT_in.ap())
            d(dbg["qT"].ap(), qT_dram.ap())
            d(dbg["att"].ap(), attT_in.ap())
            d(dbg["negm"].ap(), negm[:])
            d(dbg["ccw"].ap(), ccw_in.ap())
        ctx.wv(E, sg, sg.n)
        ctx.wv(E, C["sgw"], C["sgw"].n)
        ctx.wv(E, C["sg_q"], C["sg_q"].n)

    def f_tensor(ctx, E, ops, C):
        stn = C["stn"]
        trj = 0
        trev = []
        ctx.w(E, "consts")
        for i, (wi, ti, P) in enumerate(WTILES):
            ctx.w(E, f"wq_q{i}")
            nch = (II if wi == 2 else D) // 128
            for ci in range(nch):
                if trj >= 2:
                    ctx.w(E, trev[trj - 2])
                inst = ops.transpose(ptr[trj % 2][0:128, 0:P],
                                     wqrow[0:P, 128 * ci:128 * (ci + 1)],
                                     ident[0:P, 0:P])
                stn.inc(inst)
                ctx.mark(f"wtd{i}_{ci}", stn)
                trev.append(f"wtev{i}_{ci}")
                trj += 1
            ctx.mark(f"wq_used{i}", stn)
        for s in range(NSTRIP):
            ctx.w(E, f"ln_apply{s}")
            for ci in range(8):
                j = s * 8 + ci
                if trj >= 2:
                    ctx.w(E, trev[trj - 2])
                inst = ops.transpose(ptr[trj % 2][:],
                                     xln[s % 2][:, 128 * ci:128 * (ci + 1)],
                                     ident[:])
                stn.inc(inst)
                ctx.mark(f"xt_tr{j}", stn)
                trev.append(f"xtev{j}")
                trj += 1
        if KPH < 2:
            return
        for ip in range(NPIECE):
            ctx.w(E, f"ld_xt{ip}")
            if ip == 0:
                ctx.w(E, "w_bf16")
            for cc in range(2):
                if ip >= 1:
                    ctx.w(E, f"rope{ip - 1}_{cc}")
                inst = ops.matmul(pqk[cc][:], bqk[0:1, 128 * cc:128 * (cc + 1)],
                                  ones[0:1, :], start=True, stop=False)
                for c in range(8):
                    inst = ops.matmul(pqk[cc][:],
                                      wqk[:, c, 128 * cc:128 * (cc + 1)],
                                      xt[ip % 2][:, c, :],
                                      start=False, stop=(c == 7))
                stn.inc(inst)
                ctx.mark(f"qk{ip}_{cc}", stn)
            for t4 in range(4):
                k = ip * 4 + t4
                if k >= 2:
                    ctx.w(E, f"vev{k - 2}")
                inst = ops.matmul(pv[t4 % 2][:], ones[0:1, 0:128], bv[0:1, :],
                                  start=True, stop=False)
                for c in range(8):
                    inst = ops.matmul(pv[t4 % 2][:],
                                      xt[ip % 2][:, c, 128 * t4:128 * (t4 + 1)],
                                      wv[:, c, :], start=False, stop=(c == 7))
                stn.inc(inst)
                ctx.mark(f"v{ip}_{t4}", stn)
            ctx.mark(f"xt_used{ip}", stn)
            for cc in range(2):
                ctx.w(E, f"sq{ip}_{cc}")
                if ip + cc > 0:
                    pip, pcc = (ip, 0) if cc == 1 else (ip - 1, 1)
                    ctx.w(E, f"nred{pip}_{pcc}")
                inst = ops.matmul(pn[:], e2[:], sqb[:], start=True, stop=True)
                stn.inc(inst)
                ctx.mark(f"nmm{ip}_{cc}", stn)
        ctx.w(E, "negm_bf")
        inst = ops.matmul(pqk[0][0:128, 0:4], ones[0:1, 0:128],
                          smbf[0:1, 0:4], start=True, stop=True)
        stn.inc(inst)
        ctx.mark("negm_mm", stn)

    def f_tensor2(ctx, E, ops, C, P2, B2):
        stn = C["stn"]
        sps, avp = P2["sps"], P2["avp"]
        gun = 0
        for bsi in range(NBS):
            b, s = bsi // 16, bsi % 16
            nk = 4 * (s + 1)
            nblk = (nk + 7) // 8
            ctx.w(E, f"ld_q{bsi}")
            if bsi == 0:
                ctx.w(E, "negm_sb")
                ctx.w(E, "negm_bf")
            for blk in range(nblk):
                c0 = blk * 8
                cn = min(8, nk - c0)
                for jp in range(cn // 2):
                    pr = c0 // 2 + jp
                    for h in range(2):
                        if gun >= 2:
                            ctx.w(E, f"expu{gun - 2}")
                        for k2 in range(2):
                            kc = 2 * pr + k2
                            inst = ops.matmul(
                                sps[gun % 2][:, 512 * k2:512 * (k2 + 1)],
                                kT[64 * h:64 * (h + 1),
                                   b * T + 128 * kc:b * T + 128 * (kc + 1)],
                                B2["qstrip"][bsi % 2][64 * h:64 * (h + 1), :],
                                start=True, stop=True,
                                tile_position=(64 * h, 0))
                        stn.inc(inst)
                        ctx.mark(f"su{gun}", stn)
                        gun += 1
                for j in range(cn):
                    kc = c0 + j
                    ua = gun - cn + (j // 2) * 2
                    ctx.w(E, f"expu{ua}")
                    ctx.w(E, f"expu{ua + 1}")
                    if kc >= nk - 4:
                        ctx.w(E, f"mask{bsi}_{kc}")
                    if kc == 0 and bsi >= 2:
                        ctx.w(E, f"attmul{bsi - 2}")
                    ci = (b * T) // 128 + kc
                    for h in range(2):
                        inst = ops.matmul(avp[h][bsi % 2][:],
                                          V[:, ci, 65 * h:65 * h + 65],
                                          B2["pT"][h][blk % 2][:, j, :],
                                          start=(kc == 0), stop=(kc == nk - 1))
                    if j == cn - 1:
                        stn.inc(inst)
                        ctx.mark(f"avb{bsi}_{blk}", stn)
            ctx.mark(f"s_done{bsi}", stn)
        ctx.mark("ph2_tn", stn)

    def f_tensor3(ctx, E, ops, C, bufs, P3):
        stn = C["stn"]
        po, dn, tr3, gup = P3["po"], P3["dn"], P3["tr3"], P3["gup"]
        att_t, xq, xqT = bufs["att_t"], bufs["xq"], bufs["xqT"]
        hq_sb, hqT = bufs["hq_sb"], bufs["hqT"]
        wg_sb, wu_sb, wd_sb = bufs["wg_sb"], bufs["wu_sb"], bufs["wd_sb"]
        ctx.w(E, "ffnw")
        trg = []
        gidx = 0

        def tr_group(srcs, mark_key):
            nonlocal gidx
            if len(trg) >= 2:
                ctx.w(E, trg[-2])
            tens = tr3[gidx % 2]
            inst = None
            for k, s_ap in enumerate(srcs):
                inst = ops.transpose(tens[:, k, :], s_ap, ident[:])
            stn.inc(inst)
            ctx.mark(mark_key, stn)
            gidx += 1

        for it in range(NS3 + 1):
            prev, cur = it - 1, it
            if cur < NS3:
                ctx.w(E, f"ld_att{cur}")
                if cur >= 1:
                    ctx.w(E, f"x1_{cur - 1}")
                for half in range(2):
                    inst = ops.matmul(po[half][:], ones[0:1, 0:128],
                                      bout[0:1, 512 * half:512 * (half + 1)],
                                      start=True, stop=False)
                    for r in range(8):
                        inst = ops.matmul(po[half][:], att_t[:, r, :],
                                          wout[:, r, 512 * half:512 * (half + 1)],
                                          start=False, stop=(r == 7))
                stn.inc(inst)
                ctx.mark(f"po{cur}", stn)
            if prev >= 0:
                ctx.w(E, f"hq{prev}")
                for g in range(6):
                    ics = range(4 * g, min(4 * g + 4, 22))
                    tr_group([hq_sb[prev % 2][:, 128 * ic:128 * (ic + 1)]
                              for ic in ics], f"hqtrg{prev}_{g}")
                    trg.append(f"hqevg{prev}_{g}")
                ctx.w(E, f"hqevg{prev}_5")
                ctx.w(E, f"x1_{prev}")
                if prev >= 1:
                    ctx.w(E, f"outadd{prev - 1}")
                for ic in range(22):
                    for half in range(2):
                        inst = ops.matmul(dn[half][:], hqT[:, ic, :],
                                          wd_sb[:, ic,
                                                512 * half:512 * (half + 1)],
                                          start=(ic == 0), stop=(ic == 21))
                stn.inc(inst)
                ctx.mark(f"down{prev}", stn)
            if cur < NS3:
                ctx.w(E, f"xq{cur}")
                for g in range(2):
                    tr_group([xq[:, 128 * ci:128 * (ci + 1)]
                              for ci in range(4 * g, 4 * g + 4)],
                             f"xqtrg{cur}_{g}")
                    trg.append(f"xqevg{cur}_{g}")
                ctx.w(E, f"xqevg{cur}_1")
                for ig in range(NIG):
                    if ig >= 2:
                        ctx.w(E, f"hm{cur}_{ig - 2}")
                    elif cur >= 1:
                        ctx.w(E, f"hm{cur - 1}_{10 if ig == 0 else 9}")
                    sl = slice(IG * ig, IG * (ig + 1))
                    for c in range(8):
                        ops.matmul(gup[ig % 2][:, 0:256], xqT[:, c, :],
                                   wg_sb[:, c, sl], start=(c == 0), stop=False)
                        inst = ops.matmul(gup[ig % 2][:, 256:512],
                                          xqT[:, c, :], wu_sb[:, c, sl],
                                          start=False, stop=(c == 7))
                    stn.inc(inst)
                    ctx.mark(f"gu{cur}_{ig}", stn)

    def f_scalar(ctx, E, ops, C):
        ssc = C["ssc"]
        for i, (wi, ti, P) in enumerate(WTILES):
            nch = (II if wi == 2 else D) // 128
            if (wi, ti) == (1, 0):
                ctx.w(E, "ccw_wg_dma")
            for ci in range(nch):
                ctx.w(E, f"wtd{i}_{ci}")
                if wi < 2:
                    dst = wT_out[:, ci, 128 * ti:128 * ti + P]
                else:
                    dst = wdT_out[:, ci, 0:128]
                inst = ops.copy(dst, ptr[ci % 2][0:128, 0:P])
                ssc.inc(inst)
                ctx.mark(f"wtev{i}_{ci}", ssc)
            if (wi, ti) == (0, 2):
                ctx.mark("wT_wg_done", ssc)
            if (wi, ti) == (1, 2):
                ctx.mark("wT_wu_done", ssc)
            if wi == 2:
                ctx.mark("wT_wd_done", ssc)
        ctx.w(E, "epsb")
        for s in range(NSTRIP):
            ctx.w(E, f"bn{s}")
            inst = ops.activation(stdv[:], bn2[:, 1:2], AF.Sqrt, bias=epsb[:])
            ssc.inc(inst)
            ctx.mark(f"sqrt{s}", ssc)
            if s >= 2:
                ctx.w(E, f"cdma{s - 2}")
            for ci in range(8):
                j = s * 8 + ci
                ctx.w(E, f"xt_tr{j}")
                inst = ops.copy(cstrip[s % 2][:, ci, :], ptr[j % 2][:])
                ssc.inc(inst)
                ctx.mark(f"xtev{j}", ssc)
        ctx.mark("xln_done", ssc)
        if KPH < 2:
            return
        for ip in range(NPIECE):
            for t4 in range(4):
                k = ip * 4 + t4
                ctx.w(E, f"v{ip}_{t4}")
                ci = gcol(ip) // 128 + t4
                ops.copy(V[:, ci, 0:64], pv[t4 % 2][:, 0:64])
                inst = ops.copy(V[:, ci, 65:129], pv[t4 % 2][:, 64:128])
                ssc.inc(inst)
                ctx.mark(f"vev{k}", ssc)
        ctx.w(E, "nm_back")
        inst = ops.activation(small[0:1, 0:8], small[0:1, 0:8], AF.Sqrt)
        ssc.inc(inst)
        ctx.mark("nm_sqrt", ssc)
        ctx.w(E, "negm_mm")
        inst = ops.copy(negm[:], pqk[0][0:128, 0:4])
        ssc.inc(inst)
        ctx.mark("negm_sb", ssc)

    def f_scalar2(ctx, E, ops, C, P2, B2):
        ssc = C["ssc"]
        sps = P2["sps"]
        gun = 0
        last_pT = {0: None, 1: None}
        for bsi in range(NBS):
            b, s = bsi // 16, bsi % 16
            nk = 4 * (s + 1)
            nblk = (nk + 7) // 8
            for blk in range(nblk):
                c0 = blk * 8
                cn = min(8, nk - c0)
                if last_pT[blk % 2] is not None:
                    ctx.w(E, last_pT[blk % 2])
                for jp in range(cn // 2):
                    for h in range(2):
                        ctx.w(E, f"su{gun}")
                        inst = ops.activation(
                            B2["pT"][h][blk % 2][:, 2 * jp:2 * jp + 2, :],
                            sps[gun % 2][:], AF.Exp,
                            bias=negm[:, 2 * b + h:2 * b + h + 1],
                            scale=0.125)
                        ssc.inc(inst)
                        ctx.mark(f"expu{gun}", ssc)
                        gun += 1
                last_pT[blk % 2] = f"avb{bsi}_{blk}"
        ctx.mark("ph2_sc", ssc)

    def f_scalar3(ctx, E, ops, C, bufs, P3):
        ssc = C["ssc"]
        tr3, gup = P3["tr3"], P3["gup"]
        gidx = 0
        for it in range(NS3 + 1):
            prev, cur = it - 1, it
            if prev >= 0:
                for g in range(6):
                    n = 4 if g < 5 else 2
                    ctx.w(E, f"hqtrg{prev}_{g}")
                    inst = ops.copy(
                        bufs["hqT"][:, 4 * g:4 * g + n, :],
                        tr3[gidx % 2][:, 0:n, :])
                    gidx += 1
                    ssc.inc(inst)
                    ctx.mark(f"hqevg{prev}_{g}", ssc)
            if cur < NS3:
                ctx.w(E, f"bn3_{cur}")
                inst = ops.activation(bufs["stdv3"][:], bufs["bn2b"][:, 1:2],
                                      AF.Sqrt, bias=epsb[:])
                ssc.inc(inst)
                ctx.mark(f"sqrt3_{cur}", ssc)
                for g in range(2):
                    ctx.w(E, f"xqtrg{cur}_{g}")
                    inst = ops.copy(bufs["xqT"][:, 4 * g:4 * g + 4, :],
                                    tr3[gidx % 2][:, :, :])
                    gidx += 1
                    ssc.inc(inst)
                    ctx.mark(f"xqevg{cur}_{g}", ssc)
                for ig in range(NIG):
                    ctx.w(E, f"gu{cur}_{ig}")
                    if ig >= 2:
                        ctx.w(E, f"hm{cur}_{ig - 2}")
                    inst = ops.activation(bufs["gs"][ig % 2][:],
                                          gup[ig % 2][:, 0:256], AF.Silu)
                    ssc.inc(inst)
                    ctx.mark(f"silu{cur}_{ig}", ssc)

    def f_vector(ctx, E, ops, C):
        svc = C["svc"]
        inst = ops.memset(epsb[:], EPS)
        svc.inc(inst)
        ctx.mark("epsb", svc)
        ops.memset(V[:, :, 64:65], 1.0)
        inst = ops.memset(V[:, :, 129:130], 1.0)
        svc.inc(inst)
        ctx.mark("v_ones", svc)
        for i, (wi, ti, P) in enumerate(WTILES):
            ctx.w(E, f"ld_w{i}")
            if i > 0:
                ctx.w(E, f"wq_used{i - 1}")
            if wi == 2:
                for hh in range(2):
                    inst = emit_quant(
                        ctx, ops, E, svc,
                        wslice[0:P, 1408 * hh:1408 * (hh + 1)],
                        wqrow[0:P, 1408 * hh:1408 * (hh + 1)], P, 1408, SCR0)
                    svc.inc(inst)
            else:
                inst = emit_quant(ctx, ops, E, svc, wslice[0:P, 0:D],
                                  wqrow[0:P, 0:D], P, D, SCR0)
                svc.inc(inst)
            ctx.mark(f"wq_q{i}", svc)
            ctx.mark(f"wsl_free{i}", svc)
        for s in range(NSTRIP):
            ctx.w(E, f"ld_x{s}")
            ops.bn_stats(bn12[:, 0:6], xbuf[s % 2][:, 0:512])
            inst = ops.bn_stats(bn12[:, 6:12], xbuf[s % 2][:, 512:1024])
            svc.inc(inst)
            ctx.wv(E, svc, svc.n)
            inst = ops.bn_aggr(bn2[:], bn12[:])
            svc.inc(inst)
            ctx.mark(f"bn{s}", svc)
            ctx.w(E, f"sqrt{s}")
            inst = ops.reciprocal(rstd[:], stdv[:])
            svc.inc(inst)
            ctx.wv(E, svc, svc.n)
            if s >= 2:
                ctx.w(E, f"xtev{(s - 2) * 8 + 7}")
            inst = ops.tensor_scalar(xln[s % 2][:], xbuf[s % 2][:],
                                     bn2[:, 0:1], rstd[:],
                                     ALU.subtract, ALU.mult)
            svc.inc(inst)
            ctx.mark(f"ln_apply{s}", svc)
        if KPH < 2:
            return
        for ip in range(NPIECE):
            g = gcol(ip)
            m = (g % T) // 512
            a, w = m % 4, m // 4
            Ct = ctab[32 * a:32 * a + 32, 512 * w:512 * w + 512]
            St = stab[32 * a:32 * a + 32, 512 * w:512 * w + 512]
            for cc in range(2):
                ctx.w(E, f"qk{ip}_{cc}")
                if cc == 0 and ip >= 2:
                    ctx.w(E, f"qdma{ip - 2}")
                dstq = qTb[ip % 2][:] if cc == 0 else kT[:, g:g + 512]
                src = pqk[cc]
                for h in range(2):
                    he = src[64 * h:64 * h + 32, :]
                    ho = src[64 * h + 32:64 * h + 64, :]
                    ops.tensor_tensor(rop[0][:], he, Ct, ALU.mult)
                    ops.tensor_tensor(rop[1][:], ho, St, ALU.mult)
                    ops.tensor_tensor(dstq[64 * h:64 * h + 32, :],
                                      rop[0][:], rop[1][:], ALU.subtract)
                    ops.tensor_tensor(rop[0][:], he, St, ALU.mult)
                    ops.tensor_tensor(rop[1][:], ho, Ct, ALU.mult)
                    inst = ops.tensor_tensor(dstq[64 * h + 32:64 * h + 64, :],
                                             rop[0][:], rop[1][:], ALU.add)
                svc.inc(inst)
                ctx.mark(f"rope{ip}_{cc}", svc)
                if ip + cc > 0:
                    pip, pcc = (ip, 0) if cc == 1 else (ip - 1, 1)
                    ctx.w(E, f"nmm{pip}_{pcc}")
                inst = ops.tensor_tensor(sqb[:], dstq[0:128, 0:512],
                                         dstq[0:128, 0:512], ALU.mult)
                svc.inc(inst)
                ctx.mark(f"sq{ip}_{cc}", svc)
            for cc in range(2):
                ctx.w(E, f"nmm{ip}_{cc}")
                dst = nmq if cc == 0 else nmk
                inst = ops.tensor_reduce(dst[:, ip:ip + 1], pn[0:2, :],
                                         AX.X, ALU.max)
                svc.inc(inst)
                ctx.mark(f"nred{ip}_{cc}", svc)
        ctx.wv(E, svc, svc.n)
        for cc, srcb in [(0, nmq), (1, nmk)]:
            for b in range(2):
                inst = ops.tensor_reduce(
                    nred[:, 2 * cc + b:2 * cc + b + 1, None],
                    srcb[:, :].rearrange("q (r p) -> q r p", p=4)
                    [:, :, 2 * b:2 * b + 2],
                    AX.XY, ALU.max)
                svc.inc(inst)
        ctx.mark("nred_done", svc)
        ctx.w(E, "nm_sqrt")
        inst = ops.tensor_tensor(small[0:1, 8:12], small[0:1, 0:4],
                                 small[0:1, 4:8], ALU.mult)
        svc.inc(inst)
        ctx.wv(E, svc, svc.n)
        inst = ops.tensor_scalar(small[0:1, 12:16], small[0:1, 8:12],
                                 -0.125, None, ALU.mult)
        svc.inc(inst)
        ctx.wv(E, svc, svc.n)
        inst = ops.tensor_copy(smbf[0:1, 0:4], small[0:1, 12:16])
        svc.inc(inst)
        ctx.mark("negm_bf", svc)

    def f_vector2(ctx, E, ops, C, P2, B2):
        svc = C["svc"]
        avp = P2["avp"]
        gun = 0
        for bsi in range(NBS):
            b, s = bsi // 16, bsi % 16
            nk = 4 * (s + 1)
            gun0 = gun
            gun += nk
            if bsi == 0:
                ctx.w(E, "consts")
            for kc in range(nk - 4, nk):
                blk = kc // 8
                j = kc % 8
                dd = kc - 4 * s
                ua = gun0 + (kc // 2) * 2
                ctx.w(E, f"expu{ua}")
                ctx.w(E, f"expu{ua + 1}")
                for h in range(2):
                    inst = ops.tensor_tensor(B2["pT"][h][blk % 2][:, j, :],
                                             B2["pT"][h][blk % 2][:, j, :],
                                             mask[:, dd, :], ALU.mult)
                svc.inc(inst)
                ctx.mark(f"mask{bsi}_{kc}", svc)
            ctx.w(E, f"avb{bsi}_{(nk - 1) // 8}")
            ops.reciprocal(B2["zif"][0:1, 0:512], avp[0][bsi % 2][64:65, :])
            inst = ops.reciprocal(B2["zif"][0:1, 512:1024],
                                  avp[1][bsi % 2][64:65, :])
            svc.inc(inst)
            ctx.wv(E, svc, svc.n)
            inst = ops.tensor_copy(B2["zib"][:], B2["zif"][:])
            svc.inc(inst)
            ctx.mark(f"zi{bsi}", svc)
            ctx.wv(E, C["sg_z"], 32 * (bsi + 1))
            if bsi >= 2:
                ctx.w(E, f"attdma{bsi - 2}")
            ops.tensor_tensor(B2["attb"][bsi % 2][:, 0:512],
                              avp[0][bsi % 2][0:64, :],
                              B2["zbs"][:, 0:512], ALU.mult)
            inst = ops.tensor_tensor(B2["attb"][bsi % 2][:, 512:1024],
                                     avp[1][bsi % 2][0:64, :],
                                     B2["zbs"][:, 512:1024], ALU.mult)
            svc.inc(inst)
            ctx.mark(f"attmul{bsi}", svc)
        ctx.mark("ph2_sv", svc)

    def f_vector3(ctx, E, ops, C, bufs, P3):
        svc = C["svc"]
        po, gup = P3["po"], P3["gup"]
        x1b, xbuf3, xln2 = bufs["x1"], bufs["xbuf3"], bufs["xln2"]
        xq, gs = bufs["xq"], bufs["gs"]
        h_sb, hq_sb = bufs["h_sb"], bufs["hq_sb"]
        SCR3 = bufs["SCR3"]
        for it in range(NS3 + 1):
            prev, cur = it - 1, it
            if cur < NS3:
                x1 = x1b[cur % 2]
                ctx.w(E, f"po{cur}")
                ctx.w(E, f"ld_x3_{cur}")
                for half in range(2):
                    inst = ops.tensor_tensor(x1[:, 512 * half:512 * (half + 1)],
                                             po[half][:],
                                             xbuf3[:, 512 * half:512 * (half + 1)],
                                             ALU.add)
                svc.inc(inst)
                ctx.mark(f"x1_{cur}", svc)
                ops.bn_stats(bufs["bn12b"][:, 0:6], x1[:, 0:512])
                inst = ops.bn_stats(bufs["bn12b"][:, 6:12], x1[:, 512:1024])
                svc.inc(inst)
                ctx.wv(E, svc, svc.n)
                inst = ops.bn_aggr(bufs["bn2b"][:], bufs["bn12b"][:])
                svc.inc(inst)
                ctx.mark(f"bn3_{cur}", svc)
                ctx.w(E, f"sqrt3_{cur}")
                inst = ops.reciprocal(bufs["rstd3"][:], bufs["stdv3"][:])
                svc.inc(inst)
                ctx.wv(E, svc, svc.n)
            if prev >= 0:
                ctx.w(E, f"down{prev}")
                if prev >= 1:
                    ctx.w(E, f"outdma{prev - 1}")
                ob = bufs["out_sb"][:, 0:2048].bitcast(F32)
                for half in range(2):
                    inst = ops.tensor_tensor(
                        ob[:, 512 * half:512 * (half + 1)],
                        P3["dn"][half][:],
                        x1b[prev % 2][:, 512 * half:512 * (half + 1)], ALU.add)
                svc.inc(inst)
                ctx.mark(f"outadd{prev}", svc)
            if cur < NS3:
                x1 = x1b[cur % 2]
                inst = ops.tensor_scalar(xln2[0:128, 0:D], x1[0:128, 0:D],
                                         bufs["bn2b"][:, 0:1],
                                         bufs["rstd3"][:], ALU.subtract,
                                         ALU.mult)
                svc.inc(inst)
                ctx.wv(E, svc, svc.n)
                if cur >= 1:
                    ctx.w(E, f"xqtrg{cur - 1}_1")
                inst = emit_quant(ctx, ops, E, svc, xln2[0:128, 0:D], xq[:],
                                  128, D, SCR3)
                svc.inc(inst)
                ctx.mark(f"xq{cur}", svc)
                for ig in range(NIG):
                    if ig == 0 and prev >= 0:
                        ctx.w(E, f"outdma{prev}")
                    ctx.w(E, f"silu{cur}_{ig}")
                    inst = ops.tensor_tensor(h_sb[:, IG * ig:IG * (ig + 1)],
                                             gup[ig % 2][:, 256:512],
                                             gs[ig % 2][:], ALU.mult)
                    svc.inc(inst)
                    ctx.mark(f"hm{cur}_{ig}", svc)
                if cur >= 1:
                    ctx.w(E, f"hqtrg{cur - 1}_5")
                for hh in range(2):
                    inst = emit_quant(ctx, ops, E, svc,
                                      h_sb[:, 1408 * hh:1408 * (hh + 1)],
                                      hq_sb[cur % 2][:, 1408 * hh:1408 * (hh + 1)],
                                      128, 1408, SCR3)
                    svc.inc(inst)
                ctx.mark(f"hq{cur}", svc)
        ctx.mark("ph3_sv", svc)

    def fix_ev(EV):
        EV["wsl_freeB0"] = ("svc", 0)
        for i in range(1, len(WTILES)):
            EV[f"wsl_freeB{i}"] = EV[f"wsl_free{i - 1}"]
        if KPH < 2:
            return
        if KPH < 3:
            return

    def mk_ctrs():
        return {n: Ctr(n) for n in semnames}

    mock_ops = MockOps()

    class _MockEng:
        def wait_ge(self, *a):
            pass

        def dma_start(self, **kw):
            return MockInst()

        def collective_compute(self, *a, **kw):
            return MockInst()

    MT = _MT()
    P2r = {"sps": MT, "avp": MT}
    P3r = {"po": MT, "dn": MT, "tr3": MT, "gup": MT}
    bufsr = {k: MT for k in
             ["att_t", "xbuf3", "x1", "xln2", "xq", "xqT", "gs", "h_sb",
              "hq_sb", "hqT", "wg_sb", "wu_sb", "wd_sb", "bn12b",
              "bn2b", "stdv3", "rstd3", "rs1b", "out_sb"]}
    bufsr["SCR3"] = {k: MT for k in ["xn", "sc", "iv", "c1", "c2", "c3", "c4"]}

    def f_tail(ctx, E, C, dummy):
        sg = C["sg"]
        ctx.w(E, "tail_ready")
        for so in range(NSTRIP):
            sg.inc(E.dma_start(out=out_h.ap()[128 * so:128 * (so + 1), :],
                               in_=dummy[:]), 16)
        ctx.wv(E, sg, sg.n)

    def f_tailv(ctx, ops, C):
        svc = C["svc"]
        svc.inc(ops.memset(xbuf[0][:], 0.0))
        ctx.mark("tail_ready", svc)

    B2r = {k: MT for k in ["qstrip", "pT", "zif", "zib", "zbs", "attb"]}
    ctxr = Ctx(True, EV)
    ctxr.C = mk_ctrs()
    mke = _MockEng()
    f_sync(ctxr, mke, ctxr.C)
    if KPH >= 2:
        f_sync2(ctxr, mke, ctxr.C, B2r)
    if KPH >= 3:
        f_sync3(ctxr, mke, ctxr.C, bufsr)
    f_gpsimd(ctxr, mke, ctxr.C)
    if KPH >= 2:
        f_gpsimd2(ctxr, mke, ctxr.C, B2r)
    if KPH >= 3:
        f_gpsimd3(ctxr, mke, ctxr.C, bufsr)
    f_tensor(ctxr, mke, mock_ops, ctxr.C)
    if KPH >= 2:
        f_tensor2(ctxr, mke, mock_ops, ctxr.C, P2r, B2r)
    if KPH >= 3:
        f_tensor3(ctxr, mke, mock_ops, ctxr.C, bufsr, P3r)
    f_scalar(ctxr, mke, mock_ops, ctxr.C)
    if KPH >= 2:
        f_scalar2(ctxr, mke, mock_ops, ctxr.C, P2r, B2r)
    if KPH >= 3:
        f_scalar3(ctxr, mke, mock_ops, ctxr.C, bufsr, P3r)
    f_vector(ctxr, mke, mock_ops, ctxr.C)
    if KPH >= 2:
        f_vector2(ctxr, mke, mock_ops, ctxr.C, P2r, B2r)
    if KPH >= 3:
        f_vector3(ctxr, mke, mock_ops, ctxr.C, bufsr, P3r)
    if KPH < 3:
        f_tailv(ctxr, mock_ops, ctxr.C)
        f_tail(ctxr, mke, ctxr.C, MT)
    fix_ev(EV)
    totals = {n: c.n for n, c in ctxr.C.items()}

    ctx = Ctx(False, EV)
    ctx.C = mk_ctrs()
    for n in semnames:
        ctx.C[n].sem = handles[n]

    with nc.Block() as block:
        @block.sync
        def _(sync):
            f_sync(ctx, sync, ctx.C)

        @block.gpsimd
        def _(gp):
            f_gpsimd(ctx, gp, ctx.C)

        @block.tensor
        def _(tn):
            f_tensor(ctx, tn, nc.tensor, ctx.C)

        @block.scalar
        def _(sc_):
            f_scalar(ctx, sc_, nc.scalar, ctx.C)

        @block.vector
        def _(ve):
            f_vector(ctx, ve, nc.vector, ctx.C)

        if KPH < 2:
            @block.vector
            def _(ve):
                f_tailv(ctx, nc.vector, ctx.C)

            @block.gpsimd
            def _(gp):
                f_tail(ctx, gp, ctx.C, xbuf[0])
        es_ps1.close()
        es0.close()
        if KPH < 2:
            for n in semnames:
                assert ctx.C[n].n == totals[n], (n, ctx.C[n].n, totals[n])
            es12.close()
            es3.close()
            esg.close()
            return nc
        B2 = {
            "qstrip": [sb(es12, f"qstrip{i}", [128, 512], BF16)
                       for i in range(2)],
            "pT": [[sb(es12, f"pT{h}_{i}", [128, 8, 512], BF16)
                    for i in range(2)] for h in range(2)],
            "zif": sb(es12, "zif", [1, 1024], F32),
            "zib": sb(es12, "zib", [1, 1024], BF16),
            "zbs": sb(es12, "zbs", [64, 1024], BF16),
            "attb": [sb(es12, f"attb{i}", [64, 1024], BF16)
                     for i in range(2)],
        }
        P2 = {
            "sps": [pst(es_ps2, f"sps{i}", [128, 1024]) for i in range(2)],
            "avp": [[pst(es_ps2, f"avp{h}_{i}", [65, 512]) for i in range(2)]
                    for h in range(2)],
        }

        @block.sync
        def _(sync):
            f_sync2(ctx, sync, ctx.C, B2)

        @block.gpsimd
        def _(gp):
            f_gpsimd2(ctx, gp, ctx.C, B2)

        @block.tensor
        def _(tn):
            f_tensor2(ctx, tn, nc.tensor, ctx.C, P2, B2)

        @block.scalar
        def _(sc_):
            f_scalar2(ctx, sc_, nc.scalar, ctx.C, P2, B2)

        @block.vector
        def _(ve):
            f_vector2(ctx, ve, nc.vector, ctx.C, P2, B2)

        es_ps2.close()
        if KPH < 3:
            @block.vector
            def _(ve):
                f_tailv(ctx, nc.vector, ctx.C)

            @block.gpsimd
            def _(gp):
                f_tail(ctx, gp, ctx.C, xbuf[0])
            for n in semnames:
                assert ctx.C[n].n == totals[n], (n, ctx.C[n].n, totals[n])
            es12.close()
            es3.close()
            esg.close()
            return nc
        es12.close()
        bufs = {}
        bufs["wg_sb"] = sb(es3, "wg_sb", [128, 8, II], BF16)
        bufs["wu_sb"] = sb(es3, "wu_sb", [128, 8, II], BF16)
        bufs["wd_sb"] = sb(es3, "wd_sb", [128, 22, D], BF16)
        bufs["att_t"] = sb(es3, "att_t", [128, 8, 128], BF16)
        bufs["xbuf3"] = sb(es3, "xbuf3", [128, D], F32)
        bufs["x1"] = [sb(es3, f"x1_{i}", [128, D], F32) for i in range(2)]
        bufs["rs1b"] = sb(es3, "rs1b", [128, 1], F32)
        bufs["xq"] = sb(es3, "xq", [128, D], BF16)
        bufs["xqT"] = sb(es3, "xqT", [128, 8, 128], BF16)
        bufs["gs"] = [sb(es3, f"gs{i}", [128, IG], BF16) for i in range(2)]
        bufs["h_sb"] = sb(es3, "h_sb", [128, II], BF16)
        bufs["hq_sb"] = [sb(es3, "hq_sb0", [128, II], BF16)] * 2
        bufs["hqT"] = sb(es3, "hqT", [128, 22, 128], BF16)
        bufs["bn12b"] = sb(es3, "bn12b", [128, 12], F32)
        bufs["bn2b"] = sb(es3, "bn2b", [128, 2], F32)
        bufs["stdv3"] = sb(es3, "stdv3", [128, 1], F32)
        bufs["rstd3"] = sb(es3, "rstd3", [128, 1], F32)
        QW3 = 1408
        q3 = {"xn": sb(es3, "q3xn", [128, QW3], F32),
              "sc": sb(es3, "q3sc", [128, QW3 // 16], F32),
              "iv": sb(es3, "q3iv", [128, QW3 // 16], F32)}
        for k in ["c1", "c2", "c3", "c4"]:
            q3[k] = sb(es3, f"q3{k}", [128, QW3], BF16)
        bufs["SCR3"] = q3
        bufs["xln2"] = q3["xn"]
        bufs["out_sb"] = bufs["h_sb"]
        P3 = {
            "po": [pst(es3, f"po{i}", [128, 512]) for i in range(2)],
            "dn": [pst(es3, f"dn{i}", [128, 512]) for i in range(2)],
            "tr3": [pst(es3, f"tr3{i}", [128, 4, 128], BF16) for i in range(2)],
            "gup": [pst(es3, f"gup{i}", [128, 512]) for i in range(2)],
        }

        @block.sync
        def _(sync):
            f_sync3(ctx, sync, ctx.C, bufs)

        @block.gpsimd
        def _(gp):
            f_gpsimd3(ctx, gp, ctx.C, bufs)

        @block.tensor
        def _(tn):
            f_tensor3(ctx, tn, nc.tensor, ctx.C, bufs, P3)

        @block.scalar
        def _(sc_):
            f_scalar3(ctx, sc_, nc.scalar, ctx.C, bufs, P3)

        @block.vector
        def _(ve):
            f_vector3(ctx, ve, nc.vector, ctx.C, bufs, P3)

    for n in semnames:
        assert ctx.C[n].n == totals[n], (n, ctx.C[n].n, totals[n])
    es3.close()
    esg.close()
    return nc


_NC_CACHE = {}


def get_nc(debug=False):
    key = bool(debug)
    if key not in _NC_CACHE:
        _NC_CACHE[key] = build_nc(key)
    return _NC_CACHE[key]


def make_inputs(x, qkv_w, qkv_b, out_w, out_b, wg, wu, wd):
    inv_freq = (1.0 / 10000.0 **
                (np.arange(0, HD, 2, dtype=np.float32) / np.float32(HD))
                ).astype(np.float32)
    t = np.arange(T, dtype=np.float32)
    freqs = (t[:, None] * inv_freq[None, :]).astype(np.float32)
    rope = np.concatenate([np.sin(freqs.astype(np.float64)),
                           np.cos(freqs.astype(np.float64))], -1
                          ).astype(np.float32)
    def pack_tab(tab):
        return np.ascontiguousarray(
            tab.reshape(32, 4, 4, 512).transpose(2, 0, 1, 3).reshape(128, 2048))
    S_tab = pack_tab(np.ascontiguousarray(rope[:, 0::2].T))
    C_tab = pack_tab(np.ascontiguousarray(rope[:, 1::2].T))
    kk = np.arange(128)[:, None]
    qq = np.arange(512)[None, :]
    mask01 = np.ascontiguousarray(
        np.stack([(qq >= kk + 128 * dd).astype(np.float32)
                  for dd in range(4)], axis=1))
    ident = np.eye(128, dtype=np.float32)
    e2m = np.zeros((128, 2), np.float32)
    e2m[0:64, 0] = 1.0
    e2m[64:128, 1] = 1.0
    onesm = np.ones((1, 512), np.float32)

    in_maps = []
    for c in range(NC):
        hA, hB = 2 * c, 2 * c + 1
        perm = []
        for sec in range(2):
            for hh in (hA, hB):
                base = sec * D + hh * HD
                perm += [base + j for j in range(0, HD, 2)]
                perm += [base + j for j in range(1, HD, 2)]
        vcols = [2 * D + hh * HD + j for hh in (hA, hB) for j in range(HD)]
        xo = np.ascontiguousarray(
            x[:, 1024 * c:1024 * (c + 1), :].reshape(TOK, D))
        m = {
            "x_own": xo,
            "wqk": np.ascontiguousarray(qkv_w[:, perm]),
            "wv": np.ascontiguousarray(qkv_w[:, vcols]),
            "bqk": np.ascontiguousarray(qkv_b[perm])[None, :],
            "bv": np.ascontiguousarray(qkv_b[vcols])[None, :],
            "wout": np.ascontiguousarray(out_w),
            "bout": np.ascontiguousarray(out_b)[None, :],
            "wg_sl": np.ascontiguousarray(wg[WGR * c:WGR * (c + 1), :]),
            "wu_sl": np.ascontiguousarray(wu[WGR * c:WGR * (c + 1), :]),
            "wd_sl": np.ascontiguousarray(wd[WDR * c:WDR * (c + 1), :]),
            "ctab": C_tab, "stab": S_tab, "mask01": mask01,
            "ident": ident, "e2": e2m, "ones_in": onesm,
        }
        in_maps.append(m)
    return in_maps


def kernel(L0_attn_x, L0_attn_qkv_w, L0_attn_qkv_b, L0_attn_out_w,
           L0_attn_out_b, L0_ffn_w_gate, L0_ffn_w_up, L0_ffn_w_down):
    debug = os.environ.get("KDBG", "") == "1"
    in_maps = make_inputs(np.asarray(L0_attn_x), np.asarray(L0_attn_qkv_w),
                          np.asarray(L0_attn_qkv_b), np.asarray(L0_attn_out_w),
                          np.asarray(L0_attn_out_b), np.asarray(L0_ffn_w_gate),
                          np.asarray(L0_ffn_w_up), np.asarray(L0_ffn_w_down))
    nc = get_nc(debug)
    res = run_bass_kernel_spmd(nc, in_maps, core_ids=list(range(NC)))
    out = np.empty((B, T, D), dtype=np.float32)
    for c in range(NC):
        r = np.asarray(res.results[c]["out"], dtype=np.float32)
        out[0, 1024 * c:1024 * (c + 1), :] = r[0:1024]
        out[1, 1024 * c:1024 * (c + 1), :] = r[1024:2048]
    if debug:
        kernel.dbg = [dict(res.results[c]) for c in range(NC)]
    return out


# version 06
# speedup vs baseline: 1.3185x; 1.1034x over previous
import contextlib
import os
import numpy as np
import ml_dtypes
import concourse.bass as bass
import concourse.mybir as mybir
from concourse.bass_utils import run_bass_kernel_spmd
from concourse.aot_env import allow_ici_all_to_all

os.environ.setdefault("TRN_ALLOW_ICI_ALL_TO_ALL", "1")

F32 = mybir.dt.float32
BF16 = mybir.dt.bfloat16
U32 = mybir.dt.uint32
AF = mybir.ActivationFunctionType
ALU = mybir.AluOpType
AX = mybir.AxisListType

B, T, D, H, HD, II = 2, 8192, 1024, 16, 64, 2816
NC = 8
TOK = B * T // NC
NSTRIP = TOK // 128
WGR = II // NC
WDR = D // NC
NW = D * WGR * 2 + II * WDR
OFF_WU = D * WGR
OFF_WD = 2 * D * WGR
EPS = 1e-5
MIDBITS = [0x3E800000, 0x3F400000, 0x3FA00000, 0x3FE00000,
           0x40200000, 0x40600000, 0x40A00000]
IG = 256
NIG = II // IG
KPH = int(os.environ.get("KPHASE", "3"))
NS3 = int(os.environ.get("KSTRIP", "16"))
NPIECE = 32
NBS = 2 * 16
WTILES = [(0, 0, 128), (0, 1, 128), (0, 2, 96),
          (1, 0, 128), (1, 1, 128), (1, 2, 96),
          (2, 0, 128)]


def gcol(ip):
    r, p = ip // 4, ip % 4
    return (p // 2) * T + 1024 * r + (p % 2) * 512


class Ctr:
    def __init__(self, name):
        self.name, self.sem, self.n = name, None, 0

    def inc(self, inst, k=1):
        if inst is not None and self.sem is not None:
            inst.then_inc(self.sem, k)
        self.n += k
        return self.n


class MockInst:
    def then_inc(self, sem, k=None):
        return self


class MockOps:
    def __getattr__(self, name):
        return lambda *a, **kw: MockInst()


class _MT:
    def __getitem__(self, i):
        return self

    def __call__(self, *a, **kw):
        return self

    def __getattr__(self, name):
        return self


class Ctx:
    def __init__(self, record, EV):
        self.record = record
        self.EV = EV
        self.C = None

    def mark(self, key, ctr):
        if self.record:
            self.EV[key] = (ctr.name, ctr.n)
        else:
            assert self.EV.get(key) == (ctr.name, ctr.n), (
                "EV mismatch", key, self.EV.get(key), ctr.name, ctr.n)

    def w(self, eng, key):
        if not self.record:
            name, val = self.EV[key]
            if val > 0:
                eng.wait_ge(self.C[name].sem, val)

    def wv(self, eng, ctr, val):
        if not self.record and val > 0:
            eng.wait_ge(ctr.sem, val)


def emit_rsqrt(v, dst, var_ap, t1, t2):
    P = var_ap.shape[0]
    v.tensor_scalar(t1[0:P, :], var_ap, EPS, None, ALU.add)
    v.tensor_scalar(t2[0:P, :].bitcast(U32), t1[0:P, :].bitcast(U32),
                    -0.5, float(0x5F3759DF), ALU.mult, ALU.add)
    y = t2[0:P, :]
    for _ in range(3):
        v.tensor_tensor(dst[0:P, :], y, y, ALU.mult)
        v.tensor_tensor(dst[0:P, :], dst[0:P, :], t1[0:P, :], ALU.mult)
        v.tensor_scalar(dst[0:P, :], dst[0:P, :], -0.5, 1.5, ALU.mult, ALU.add)
        v.tensor_tensor(t2[0:P, :], y, dst[0:P, :], ALU.mult)
    return v.tensor_copy(dst[0:P, :], t2[0:P, :])


def emit_quant(ctx, v, eng, svc, src_ap, dst_ap, P, W, scr):
    nb = W // 16
    xn, sc, iv = scr["xn"], scr["sc"], scr["iv"]
    c1, c2, c3, c4 = scr["c1"], scr["c2"], scr["c3"], scr["c4"]
    svc.inc(v.tensor_reduce(sc[0:P, 0:nb, None],
                            src_ap.rearrange("p (b k) -> p b k", k=16),
                            AX.X, ALU.max, apply_absolute_value=True))
    ctx.wv(eng, svc, svc.n)
    svc.inc(v.tensor_scalar(sc[0:P, 0:nb], sc[0:P, 0:nb], 1.0 / 6.0, 1e-12,
                            ALU.mult, ALU.max))
    ctx.wv(eng, svc, svc.n)
    svc.inc(v.reciprocal(iv[0:P, 0:nb], sc[0:P, 0:nb]))
    ctx.wv(eng, svc, svc.n)
    v.tensor_tensor(xn[0:P, 0:W].rearrange("p (b k) -> p b k", k=16),
                    src_ap.rearrange("p (b k) -> p b k", k=16),
                    iv[0:P, 0:nb, None].broadcast_to([P, nb, 16]), ALU.mult)
    xn_u = xn[0:P, 0:W].bitcast(U32)
    v.tensor_scalar(c4[0:P, 0:W], xn[0:P, 0:W], 0.0, None, ALU.is_lt)
    v.tensor_scalar(c4[0:P, 0:W], c4[0:P, 0:W], -2.0, 1.0, ALU.mult, ALU.add)
    v.tensor_scalar(xn_u, xn_u, 0x7FFFFFFF, None, ALU.bitwise_and)
    v.tensor_scalar(c1[0:P, 0:W], xn_u, MIDBITS[0], None, ALU.is_gt)
    v.tensor_scalar(c2[0:P, 0:W], xn_u, MIDBITS[1], None, ALU.is_gt)
    v.tensor_tensor(c1[0:P, 0:W], c1[0:P, 0:W], c2[0:P, 0:W], ALU.add)
    v.tensor_scalar(c2[0:P, 0:W], xn_u, MIDBITS[2], None, ALU.is_gt)
    v.tensor_scalar(c3[0:P, 0:W], xn_u, MIDBITS[3], None, ALU.is_gt)
    v.tensor_tensor(c2[0:P, 0:W], c2[0:P, 0:W], c3[0:P, 0:W], ALU.add)
    v.tensor_tensor(c1[0:P, 0:W], c1[0:P, 0:W], c2[0:P, 0:W], ALU.add)
    v.tensor_scalar(c1[0:P, 0:W], c1[0:P, 0:W], 0.5, None, ALU.mult)
    v.tensor_scalar(c2[0:P, 0:W], xn_u, MIDBITS[4], None, ALU.is_gt)
    v.tensor_scalar(c3[0:P, 0:W], xn_u, MIDBITS[5], None, ALU.is_gt)
    v.tensor_tensor(c2[0:P, 0:W], c2[0:P, 0:W], c3[0:P, 0:W], ALU.add)
    v.tensor_tensor(c1[0:P, 0:W], c1[0:P, 0:W], c2[0:P, 0:W], ALU.add)
    v.tensor_scalar(c2[0:P, 0:W], xn_u, MIDBITS[6], None, ALU.is_gt)
    v.tensor_scalar(c2[0:P, 0:W], c2[0:P, 0:W], 2.0, None, ALU.mult)
    v.tensor_tensor(c1[0:P, 0:W], c1[0:P, 0:W], c2[0:P, 0:W], ALU.add)
    v.tensor_tensor(c1[0:P, 0:W], c1[0:P, 0:W], c4[0:P, 0:W], ALU.mult)
    return v.tensor_tensor(dst_ap.rearrange("p (b k) -> p b k", k=16),
                           c1[0:P, 0:W].rearrange("p (b k) -> p b k", k=16),
                           sc[0:P, 0:nb, None].broadcast_to([P, nb, 16]),
                           ALU.mult)


def build_nc(debug=False):
    nc = bass.Bass("TRN2", target_bir_lowering=False)

    x_h = nc.dram_tensor("x_own", [TOK, D], F32, kind="ExternalInput")
    wqk_h = nc.dram_tensor("wqk", [D, 256], F32, kind="ExternalInput")
    wv_h = nc.dram_tensor("wv", [D, 128], F32, kind="ExternalInput")
    bqk_h = nc.dram_tensor("bqk", [1, 256], F32, kind="ExternalInput")
    bv_h = nc.dram_tensor("bv", [1, 128], F32, kind="ExternalInput")
    wout_h = nc.dram_tensor("wout", [D, D], F32, kind="ExternalInput")
    bout_h = nc.dram_tensor("bout", [1, D], F32, kind="ExternalInput")
    wg_h = nc.dram_tensor("wg_sl", [WGR, D], F32, kind="ExternalInput")
    wu_h = nc.dram_tensor("wu_sl", [WGR, D], F32, kind="ExternalInput")
    wd_h = nc.dram_tensor("wd_sl", [WDR, II], F32, kind="ExternalInput")
    ctab_h = nc.dram_tensor("ctab", [128, 2048], F32, kind="ExternalInput")
    stab_h = nc.dram_tensor("stab", [128, 2048], F32, kind="ExternalInput")
    mask_h = nc.dram_tensor("mask01", [128, 4, 512], F32, kind="ExternalInput")
    id_h = nc.dram_tensor("ident", [128, 128], F32, kind="ExternalInput")
    e2_h = nc.dram_tensor("e2", [128, 2], F32, kind="ExternalInput")
    one_h = nc.dram_tensor("ones_in", [1, 512], F32, kind="ExternalInput")
    out_h = nc.dram_tensor("out", [TOK, D], F32, kind="ExternalOutput")

    xlnT_in = nc.dram_tensor("xlnT_in", [D, TOK], BF16)
    xlnT_all = nc.dram_tensor("xlnT_all", [NC * D, TOK], BF16)
    qT_dram = nc.dram_tensor("qT_dram", [128, B * T], BF16)
    attT_in = nc.dram_tensor("attT_in", [NC * 128, TOK], BF16)
    attT_out = nc.dram_tensor("attT_out", [NC * 128, TOK], BF16)
    ccw_in = nc.dram_tensor("ccw_in", [NW], BF16)
    ccw_all = nc.dram_tensor("ccw_all", [NC, NW], BF16)
    nm_dram = nc.dram_tensor("nm_dram", [8], F32)
    zscr = nc.dram_tensor("zscr", [1024], BF16)

    dbg = {}
    if debug:
        dbg["xlnT"] = nc.dram_tensor("d_xlnT", [D, TOK], BF16, kind="ExternalOutput")
        dbg["qT"] = nc.dram_tensor("d_qT", [128, B * T], BF16, kind="ExternalOutput")
        dbg["att"] = nc.dram_tensor("d_att", [NC * 128, TOK], BF16, kind="ExternalOutput")
        dbg["negm"] = nc.dram_tensor("d_negm", [128, 4], F32, kind="ExternalOutput")
        dbg["ccw"] = nc.dram_tensor("d_ccw", [NW], BF16, kind="ExternalOutput")

    esg = contextlib.ExitStack()
    es12 = contextlib.ExitStack()
    es0 = contextlib.ExitStack()
    es3 = contextlib.ExitStack()
    es_ps1 = contextlib.ExitStack()
    es_ps2 = contextlib.ExitStack()

    def sb(es, name, shape, dt):
        return es.enter_context(nc.sbuf_tensor(name, shape, dt))

    def pst(es, name, shape, dt=F32):
        return es.enter_context(nc.psum_tensor(name, shape, dt))

    ident = sb(esg, "ident_s", [128, 128], BF16)
    ones = sb(esg, "ones_s", [1, 512], BF16)
    e2 = sb(esg, "e2_s", [128, 2], BF16)
    wout = sb(esg, "wout_s", [128, 8, D], BF16)
    bout = sb(esg, "bout_s", [1, D], BF16)
    negm = sb(esg, "negm_s", [128, 4], F32)
    small = sb(esg, "small_s", [1, 16], F32)
    epsb = sb(esg, "epsb_s", [128, 1], F32)
    smbf = sb(esg, "smbf_s", [1, 8], BF16)

    semnames = ["sd", "sg", "scc", "stn", "ssc", "svc",
                "s_ws", "s_x0", "s_x1", "s_xt0", "s_xt1", "s_q0", "s_q1",
                "s_att", "s_x3", "sgw", "sg_c0", "sg_c1", "sg_q",
                "sg_a0", "sg_a1", "sg_z"]
    handles = {n: esg.enter_context(nc.semaphore(n)) for n in semnames}
    EV = {}

    wqk = sb(es12, "wqk_s", [128, 8, 256], BF16)
    wv = sb(es12, "wv_s", [128, 8, 128], BF16)
    bqk = sb(es12, "bqk_s", [1, 256], BF16)
    bv = sb(es12, "bv_s", [1, 128], BF16)
    mask = sb(es12, "mask_s", [128, 4, 512], BF16)
    ctab = sb(es12, "ctab_s", [128, 2048], F32)
    stab = sb(es12, "stab_s", [128, 2048], F32)
    xt = [sb(es12, f"xt{i}", [128, 8, 512], BF16) for i in range(2)]
    qTb = [sb(es12, f"qTb{i}", [128, 512], BF16) for i in range(2)]
    kT = sb(es12, "kT", [128, B * T], BF16)
    V = sb(es12, "V", [128, 128, 130], BF16)
    rop = [sb(es12, f"rop{i}", [32, 512], F32) for i in range(2)]
    sqb = sb(es12, "sqb", [128, 512], BF16)
    nmq = sb(es12, "nmq", [2, 32], F32)
    nmk = sb(es12, "nmk", [2, 32], F32)
    nred = sb(es12, "nred", [2, 4], F32)
    xbuf = [sb(es0, f"xbuf{i}", [128, D], F32) for i in range(2)]
    xln = [sb(es0, f"xln{i}", [128, D], BF16) for i in range(2)]
    bn12 = sb(es0, "bn12", [128, 12], F32)
    bn2 = sb(es0, "bn2", [128, 2], F32)
    stdv = sb(es0, "stdv", [128, 1], F32)
    rstd = sb(es0, "rstd", [128, 1], F32)
    q_rs1 = sb(es0, "q_rs1", [128, 1], F32)
    cstrip = [sb(es0, f"cstrip{i}", [128, 8, 128], BF16) for i in range(2)]
    QW = 1408
    SCR0 = {
        "xn": sb(es0, "q_xn", [128, QW], F32),
        "sc": sb(es0, "q_sc", [128, QW // 16], F32),
        "iv": sb(es0, "q_iv", [128, QW // 16], F32),
        "c1": sb(es0, "q_c1", [128, QW], BF16),
        "c2": sb(es0, "q_c2", [128, QW], BF16),
        "c3": sb(es0, "q_c3", [128, QW], BF16),
        "c4": sb(es0, "q_c4", [128, QW], BF16),
    }
    wslice = sb(es0, "wslice", [128, II], F32)
    wqrow = sb(es0, "wqrow", [128, II], BF16)
    wT_out = sb(es0, "wT_out", [128, 8, WGR], BF16)
    wdT_out = sb(es0, "wdT_out", [128, 22, 128], BF16)
    pqk = [pst(es_ps1, f"pqk{i}", [128, 512]) for i in range(2)]
    pv = [pst(es_ps1, f"pv{i}", [128, 128]) for i in range(2)]
    pn = pst(es_ps1, "pn", [2, 512])
    ptr = [pst(es_ps1, f"ptr{i}", [128, 128], BF16) for i in range(2)]


    def f_sync(ctx, E, C):
        def d(ctr, dst, src):
            return ctr.inc(E.dma_start(out=dst, in_=src), 16)

        d(C["s_ws"], ctab[:], ctab_h.ap())
        d(C["s_ws"], stab[:], stab_h.ap())
        for i, (wi, ti, P) in enumerate(WTILES):
            ctx.w(E, f"wsl_freeB{i}")
            wh = [wg_h, wu_h, wd_h][wi]
            WR = II if wi == 2 else D
            d(C["s_ws"], wslice[0:P, 0:WR], wh.ap()[128 * ti:128 * ti + P, :])
            ctx.mark(f"ld_w{i}", C["s_ws"])
        for s in range(NSTRIP):
            if s >= 2:
                ctx.w(E, f"ln_apply{s - 2}")
            d(C[f"s_x{s % 2}"], xbuf[s % 2][:], x_h.ap()[128 * s:128 * (s + 1), :])
            ctx.mark(f"ld_x{s}", C[f"s_x{s % 2}"])
        if KPH < 2:
            return
        for ip in range(NPIECE):
            r, p = ip // 4, ip % 4
            if ip == 0:
                ctx.wv(E, C["scc"], 2)
            if ip >= 2:
                ctx.w(E, f"xt_used{ip - 2}")
            d(C[f"s_xt{ip % 2}"], xt[ip % 2][:],
              xlnT_all.ap()[D * r:D * (r + 1), 512 * p:512 * (p + 1)]
              .rearrange("(c p) t -> p c t", p=128))
            ctx.mark(f"ld_xt{ip}", C[f"s_xt{ip % 2}"])
    def f_sync2(ctx, E, C, B2):
        def d(ctr, dst, src):
            return ctr.inc(E.dma_start(out=dst, in_=src), 16)

        for bsi in range(NBS):
            b, s = bsi // 16, bsi % 16
            if bsi == 0:
                ctx.wv(E, C["sg_q"], 16 * NPIECE)
            if bsi >= 2:
                ctx.w(E, f"s_done{bsi - 2}")
            d(C[f"s_q{bsi % 2}"], B2["qstrip"][bsi % 2][:],
              qT_dram.ap()[:, b * T + 512 * s: b * T + 512 * (s + 1)])
            ctx.mark(f"ld_q{bsi}", C[f"s_q{bsi % 2}"])

    def f_sync3(ctx, E, C, bufs):
        def d(ctr, dst, src):
            return ctr.inc(E.dma_start(out=dst, in_=src), 16)

        for so in range(NS3):
            if so == 0:
                ctx.wv(E, C["scc"], 3)
                ctx.w(E, "ph2_tn")
                ctx.w(E, "ph2_sv")
                ctx.w(E, "ph2_sc")
            if so >= 1:
                ctx.w(E, f"po{so - 1}")
            d(C["s_att"], bufs["att_t"][:],
              attT_out.ap().rearrange("(r p) t -> p r t", p=128)
              [:, :, 128 * so:128 * (so + 1)])
            ctx.mark(f"ld_att{so}", C["s_att"])
            if so >= 1:
                ctx.w(E, f"x1_{so - 1}")
            d(C["s_x3"], bufs["xbuf3"][:], x_h.ap()[128 * so:128 * (so + 1), :])
            ctx.mark(f"ld_x3_{so}", C["s_x3"])

    def f_gpsimd(ctx, E, C):
        sg, sgw, scc = C["sg"], C["sgw"], C["scc"]

        def d(ctr, dst, src):
            return ctr.inc(E.dma_start(out=dst, in_=src), 16)

        d(sgw, ident[:], id_h.ap())
        d(sgw, ones[:], one_h.ap())
        d(sgw, e2[:], e2_h.ap())
        d(sgw, mask[:], mask_h.ap())
        d(sgw, wqk[:], wqk_h.ap().rearrange("(c p) n -> p c n", p=128))
        d(sgw, wv[:], wv_h.ap().rearrange("(c p) n -> p c n", p=128))
        d(sgw, bqk[:], bqk_h.ap())
        d(sgw, bv[:], bv_h.ap())
        d(sgw, bout[:], bout_h.ap())
        d(sgw, wout[:], wout_h.ap().rearrange("(c p) n -> p c n", p=128))
        ctx.mark("w_bf16", sgw)
        ctx.mark("consts", sgw)
        ctx.w(E, "wT_wg_done")
        d(sg, ccw_in.ap()[0:OFF_WU].rearrange("(c p i) -> p c i", p=128, i=WGR),
          wT_out[:])
        ctx.mark("ccw_wg_dma", sg)
        ctx.w(E, "wT_wu_done")
        d(sg, ccw_in.ap()[OFF_WU:OFF_WD].rearrange("(c p i) -> p c i", p=128, i=WGR),
          wT_out[:])
        ctx.mark("ccw_wu_dma", sg)
        ctx.w(E, "wT_wd_done")
        d(sg, ccw_in.ap()[OFF_WD:NW].rearrange("(c p h) -> p c h", p=128, h=128),
          wdT_out[:])
        ctx.wv(E, sg, sg.n)
        scc.inc(E.collective_compute(
            "AllGather", ALU.bypass, replica_groups=[list(range(NC))],
            ins=[ccw_in.ap().opt()], outs=[ccw_all.ap().opt()]))
        for s in range(NSTRIP):
            ctx.w(E, f"xtev{s * 8 + 7}")
            d(C[f"sg_c{s % 2}"], xlnT_in.ap().rearrange("(c p) t -> p c t", p=128)
              [:, :, 128 * s:128 * (s + 1)], cstrip[s % 2][:])
            ctx.mark(f"cdma{s}", C[f"sg_c{s % 2}"])
        ctx.wv(E, C["sg_c0"], C["sg_c0"].n)
        ctx.wv(E, C["sg_c1"], C["sg_c1"].n)
        scc.inc(E.collective_compute(
            "AllGather", ALU.bypass, replica_groups=[list(range(NC))],
            ins=[xlnT_in.ap().opt()], outs=[xlnT_all.ap().opt()]))
        if KPH < 2:
            return
        for ip in range(NPIECE):
            ctx.w(E, f"rope{ip}_0")
            g = gcol(ip)
            d(C["sg_q"], qT_dram.ap()[:, g:g + 512], qTb[ip % 2][:])
            ctx.mark(f"qdma{ip}", C["sg_q"])
        ctx.w(E, "nred_done")
        sg.inc(E.dma_start(out=nm_dram.ap()[0:4].rearrange("(b h) -> h b", h=2),
                           in_=nred[0:2, 0:2],
                           allow_slow_non_contiguous=True), 16)
        sg.inc(E.dma_start(out=nm_dram.ap()[4:8].rearrange("(b h) -> h b", h=2),
                           in_=nred[0:2, 2:4],
                           allow_slow_non_contiguous=True), 16)
        ctx.wv(E, sg, sg.n)
        d(sg, small[0:1, 0:8], nm_dram.ap()[None, :])
        ctx.mark("nm_back", sg)
    def f_gpsimd2(ctx, E, C, B2):
        scc = C["scc"]

        def d(ctr, dst, src):
            return ctr.inc(E.dma_start(out=dst, in_=src), 16)

        sgz = C["sg_z"]
        for bsi in range(NBS):
            b, s = bsi // 16, bsi % 16
            o = s // 2
            oc = b * 1024 + (s % 2) * 512
            ctx.w(E, f"zi{bsi}")
            d(sgz, zscr.ap()[None, :], B2["zib"][:])
            ctx.wv(E, sgz, sgz.n)
            d(sgz, B2["zbs"][:],
              zscr.ap()[None, :].broadcast_to([64, 1024]))
            ctx.w(E, f"attmul{bsi}")
            d(C[f"sg_a{bsi % 2}"], attT_in.ap()[128 * o:128 * (o + 1), oc:oc + 512]
              .rearrange("(j p) t -> p j t", p=64),
              B2["attb"][bsi % 2][:].rearrange("p (j t) -> p j t", j=2))
            ctx.mark(f"attdma{bsi}", C[f"sg_a{bsi % 2}"])
        ctx.wv(E, C["sg_a0"], C["sg_a0"].n)
        ctx.wv(E, C["sg_a1"], C["sg_a1"].n)
        with allow_ici_all_to_all():
            scc.inc(E.collective_compute(
                "AllToAll", ALU.bypass, replica_groups=[list(range(NC))],
                ins=[attT_in.ap().opt()], outs=[attT_out.ap().opt()]))

    def f_gpsimd3(ctx, E, C, bufs):
        sg = C["sg"]

        def d(dst, src):
            return sg.inc(E.dma_start(out=dst, in_=src), 16)

        ctx.wv(E, C["sg_a0"], C["sg_a0"].n)
        ctx.wv(E, C["sg_a1"], C["sg_a1"].n)
        ctx.wv(E, C["scc"], 3)
        ctx.w(E, "ph2_tn")
        ctx.w(E, "ph2_sv")
        ctx.w(E, "ph2_sc")
        for c in range(8):
            d(bufs["wg_sb"][:, c, :].rearrange("p (r i) -> p r i", i=WGR),
              ccw_all.ap()[:, 128 * c * WGR:(128 * c + 128) * WGR]
              .rearrange("r (p i) -> p r i", p=128))
            d(bufs["wu_sb"][:, c, :].rearrange("p (r i) -> p r i", i=WGR),
              ccw_all.ap()[:, OFF_WU + 128 * c * WGR:
                           OFF_WU + (128 * c + 128) * WGR]
              .rearrange("r (p i) -> p r i", p=128))
        for ic in range(22):
            d(bufs["wd_sb"][:, ic, :].rearrange("p (r h) -> p r h", h=128),
              ccw_all.ap()[:, OFF_WD + 128 * ic * 128:
                           OFF_WD + (128 * ic + 128) * 128]
              .rearrange("r (p h) -> p r h", p=128))
        ctx.mark("ffnw", sg)
        for so in range(NSTRIP):
            if so < NS3:
                ctx.w(E, f"outadd{so}")
            d(out_h.ap()[128 * so:128 * (so + 1), :],
              bufs["out_sb"][:, 0:2048].bitcast(F32))
            if so < NS3:
                ctx.mark(f"outdma{so}", sg)
        if debug:
            d(dbg["xlnT"].ap(), xlnT_in.ap())
            d(dbg["qT"].ap(), qT_dram.ap())
            d(dbg["att"].ap(), attT_in.ap())
            d(dbg["negm"].ap(), negm[:])
            d(dbg["ccw"].ap(), ccw_in.ap())
        ctx.wv(E, sg, sg.n)
        ctx.wv(E, C["sgw"], C["sgw"].n)
        ctx.wv(E, C["sg_q"], C["sg_q"].n)

    def f_tensor(ctx, E, ops, C):
        stn = C["stn"]
        trj = 0
        trev = []
        ctx.w(E, "consts")
        for i, (wi, ti, P) in enumerate(WTILES):
            ctx.w(E, f"wq_q{i}")
            nch = (II if wi == 2 else D) // 128
            for ci in range(nch):
                if trj >= 2:
                    ctx.w(E, trev[trj - 2])
                inst = ops.transpose(ptr[trj % 2][0:128, 0:P],
                                     wqrow[0:P, 128 * ci:128 * (ci + 1)],
                                     ident[0:P, 0:P])
                stn.inc(inst)
                ctx.mark(f"wtd{i}_{ci}", stn)
                trev.append(f"wtev{i}_{ci}")
                trj += 1
            ctx.mark(f"wq_used{i}", stn)
        for s in range(NSTRIP):
            ctx.w(E, f"ln_apply{s}")
            for ci in range(8):
                j = s * 8 + ci
                if trj >= 2:
                    ctx.w(E, trev[trj - 2])
                inst = ops.transpose(ptr[trj % 2][:],
                                     xln[s % 2][:, 128 * ci:128 * (ci + 1)],
                                     ident[:])
                stn.inc(inst)
                ctx.mark(f"xt_tr{j}", stn)
                trev.append(f"xtev{j}")
                trj += 1
        if KPH < 2:
            return
        for ip in range(NPIECE):
            ctx.w(E, f"ld_xt{ip}")
            if ip == 0:
                ctx.w(E, "w_bf16")
            for cc in range(2):
                if ip >= 1:
                    ctx.w(E, f"rope{ip - 1}_{cc}")
                inst = ops.matmul(pqk[cc][:], bqk[0:1, 128 * cc:128 * (cc + 1)],
                                  ones[0:1, :], start=True, stop=False)
                for c in range(8):
                    inst = ops.matmul(pqk[cc][:],
                                      wqk[:, c, 128 * cc:128 * (cc + 1)],
                                      xt[ip % 2][:, c, :],
                                      start=False, stop=(c == 7))
                stn.inc(inst)
                ctx.mark(f"qk{ip}_{cc}", stn)
            for t4 in range(4):
                k = ip * 4 + t4
                if k >= 2:
                    ctx.w(E, f"vev{k - 2}")
                inst = ops.matmul(pv[t4 % 2][:], ones[0:1, 0:128], bv[0:1, :],
                                  start=True, stop=False)
                for c in range(8):
                    inst = ops.matmul(pv[t4 % 2][:],
                                      xt[ip % 2][:, c, 128 * t4:128 * (t4 + 1)],
                                      wv[:, c, :], start=False, stop=(c == 7))
                stn.inc(inst)
                ctx.mark(f"v{ip}_{t4}", stn)
            ctx.mark(f"xt_used{ip}", stn)
            for cc in range(2):
                ctx.w(E, f"sq{ip}_{cc}")
                if ip + cc > 0:
                    pip, pcc = (ip, 0) if cc == 1 else (ip - 1, 1)
                    ctx.w(E, f"nred{pip}_{pcc}")
                inst = ops.matmul(pn[:], e2[:], sqb[:], start=True, stop=True)
                stn.inc(inst)
                ctx.mark(f"nmm{ip}_{cc}", stn)
        ctx.w(E, "negm_bf")
        inst = ops.matmul(pqk[0][0:128, 0:4], ones[0:1, 0:128],
                          smbf[0:1, 0:4], start=True, stop=True)
        stn.inc(inst)
        ctx.mark("negm_mm", stn)

    def f_tensor2(ctx, E, ops, C, P2, B2):
        stn = C["stn"]
        sps, avp = P2["sps"], P2["avp"]
        gun = 0

        def emit_su(bsi, b, pr, h):
            nonlocal gun
            if gun >= 2:
                ctx.w(E, f"expu{gun - 2}")
            for k2 in range(2):
                kc = 2 * pr + k2
                inst = ops.matmul(
                    sps[gun % 2][:, 512 * k2:512 * (k2 + 1)],
                    kT[64 * h:64 * (h + 1),
                       b * T + 128 * kc:b * T + 128 * (kc + 1)],
                    B2["qstrip"][bsi % 2][64 * h:64 * (h + 1), :],
                    start=True, stop=True, tile_position=(64 * h, 0))
            stn.inc(inst)
            ctx.mark(f"su{gun}", stn)
            gun += 1

        def emit_av(bsi, b, s, nk, kc, sblk_gun0):
            blk = kc // 8
            j = kc % 8
            ua = sblk_gun0[blk] + (j // 2) * 2
            ctx.w(E, f"expu{ua}")
            ctx.w(E, f"expu{ua + 1}")
            if kc >= nk - 4:
                ctx.w(E, f"mask{bsi}_{kc}")
            if kc == 0 and bsi >= 2:
                ctx.w(E, f"attmul{bsi - 2}")
            ci = (b * T) // 128 + kc
            for h in range(2):
                inst = ops.matmul(avp[h][bsi % 2][:],
                                  V[:, ci, 65 * h:65 * h + 65],
                                  B2["pT"][h][blk % 2][:, j, :],
                                  start=(kc == 0), stop=(kc == nk - 1))
            if kc % 8 == min(7, nk - 1 - blk * 8):
                stn.inc(inst)
                ctx.mark(f"avb{bsi}_{blk}", stn)

        for bsi in range(NBS):
            b, s = bsi // 16, bsi % 16
            nk = 4 * (s + 1)
            nblk = (nk + 7) // 8
            ctx.w(E, f"ld_q{bsi}")
            if bsi == 0:
                ctx.w(E, "negm_sb")
                ctx.w(E, "negm_bf")
            sblk_gun0 = {}
            units = []
            for blk in range(nblk):
                c0 = blk * 8
                cn = min(8, nk - c0)
                sblk_gun0[blk] = None
                for jp in range(cn // 2):
                    units.append((blk, c0 // 2 + jp))
            ui = 0
            n0 = min(8, nk) // 2
            for _ in range(n0):
                blk, pr = units[ui]
                if sblk_gun0[blk] is None:
                    sblk_gun0[blk] = gun
                emit_su(bsi, b, pr, 0)
                emit_su(bsi, b, pr, 1)
                ui += 1
            for kc in range(nk):
                if kc % 2 == 0 and ui < len(units):
                    blk, pr = units[ui]
                    if sblk_gun0[blk] is None:
                        sblk_gun0[blk] = gun
                    emit_su(bsi, b, pr, 0)
                    emit_su(bsi, b, pr, 1)
                    ui += 1
                emit_av(bsi, b, s, nk, kc, sblk_gun0)
            ctx.mark(f"s_done{bsi}", stn)
        ctx.mark("ph2_tn", stn)

    def f_tensor3(ctx, E, ops, C, bufs, P3):
        stn = C["stn"]
        po, dn, tr3, gup = P3["po"], P3["dn"], P3["tr3"], P3["gup"]
        att_t, xq, xqT = bufs["att_t"], bufs["xq"], bufs["xqT"]
        hq_sb, hqT = bufs["hq_sb"], bufs["hqT"]
        wg_sb, wu_sb, wd_sb = bufs["wg_sb"], bufs["wu_sb"], bufs["wd_sb"]
        ctx.w(E, "ffnw")
        trg = []
        gidx = 0

        def tr_group(srcs, mark_key):
            nonlocal gidx
            if len(trg) >= 2:
                ctx.w(E, trg[-2])
            tens = tr3[gidx % 2]
            inst = None
            for k, s_ap in enumerate(srcs):
                inst = ops.transpose(tens[:, k, :], s_ap, ident[:])
            stn.inc(inst)
            ctx.mark(mark_key, stn)
            gidx += 1

        for it in range(NS3 + 1):
            prev, cur = it - 1, it
            if cur < NS3:
                ctx.w(E, f"ld_att{cur}")
                if cur >= 1:
                    ctx.w(E, f"x1_{cur - 1}")
                for half in range(2):
                    inst = ops.matmul(po[half][:], ones[0:1, 0:128],
                                      bout[0:1, 512 * half:512 * (half + 1)],
                                      start=True, stop=False)
                    for r in range(8):
                        inst = ops.matmul(po[half][:], att_t[:, r, :],
                                          wout[:, r, 512 * half:512 * (half + 1)],
                                          start=False, stop=(r == 7))
                stn.inc(inst)
                ctx.mark(f"po{cur}", stn)
            if prev >= 0:
                ctx.w(E, f"hq{prev}")
                for g in range(6):
                    ics = range(4 * g, min(4 * g + 4, 22))
                    tr_group([hq_sb[prev % 2][:, 128 * ic:128 * (ic + 1)]
                              for ic in ics], f"hqtrg{prev}_{g}")
                    trg.append(f"hqevg{prev}_{g}")
                ctx.w(E, f"hqevg{prev}_5")
                ctx.w(E, f"x1_{prev}")
                if prev >= 1:
                    ctx.w(E, f"outadd{prev - 1}")
                for ic in range(22):
                    for half in range(2):
                        inst = ops.matmul(dn[half][:], hqT[:, ic, :],
                                          wd_sb[:, ic,
                                                512 * half:512 * (half + 1)],
                                          start=(ic == 0), stop=(ic == 21))
                stn.inc(inst)
                ctx.mark(f"down{prev}", stn)
            if cur < NS3:
                ctx.w(E, f"xq{cur}")
                for g in range(2):
                    tr_group([xq[:, 128 * ci:128 * (ci + 1)]
                              for ci in range(4 * g, 4 * g + 4)],
                             f"xqtrg{cur}_{g}")
                    trg.append(f"xqevg{cur}_{g}")
                ctx.w(E, f"xqevg{cur}_1")
                for ig in range(NIG):
                    if ig >= 2:
                        ctx.w(E, f"hm{cur}_{ig - 2}")
                    elif cur >= 1:
                        ctx.w(E, f"hm{cur - 1}_{10 if ig == 0 else 9}")
                    sl = slice(IG * ig, IG * (ig + 1))
                    for c in range(8):
                        ops.matmul(gup[ig % 2][:, 0:256], xqT[:, c, :],
                                   wg_sb[:, c, sl], start=(c == 0), stop=False)
                        inst = ops.matmul(gup[ig % 2][:, 256:512],
                                          xqT[:, c, :], wu_sb[:, c, sl],
                                          start=False, stop=(c == 7))
                    stn.inc(inst)
                    ctx.mark(f"gu{cur}_{ig}", stn)

    def f_scalar(ctx, E, ops, C):
        ssc = C["ssc"]
        for i, (wi, ti, P) in enumerate(WTILES):
            nch = (II if wi == 2 else D) // 128
            if (wi, ti) == (1, 0):
                ctx.w(E, "ccw_wg_dma")
            for ci in range(nch):
                ctx.w(E, f"wtd{i}_{ci}")
                if wi < 2:
                    dst = wT_out[:, ci, 128 * ti:128 * ti + P]
                else:
                    dst = wdT_out[:, ci, 0:128]
                inst = ops.copy(dst, ptr[ci % 2][0:128, 0:P])
                ssc.inc(inst)
                ctx.mark(f"wtev{i}_{ci}", ssc)
            if (wi, ti) == (0, 2):
                ctx.mark("wT_wg_done", ssc)
            if (wi, ti) == (1, 2):
                ctx.mark("wT_wu_done", ssc)
            if wi == 2:
                ctx.mark("wT_wd_done", ssc)
        ctx.w(E, "epsb")
        for s in range(NSTRIP):
            ctx.w(E, f"bn{s}")
            inst = ops.activation(stdv[:], bn2[:, 1:2], AF.Sqrt, bias=epsb[:])
            ssc.inc(inst)
            ctx.mark(f"sqrt{s}", ssc)
            if s >= 2:
                ctx.w(E, f"cdma{s - 2}")
            for ci in range(8):
                j = s * 8 + ci
                ctx.w(E, f"xt_tr{j}")
                inst = ops.copy(cstrip[s % 2][:, ci, :], ptr[j % 2][:])
                ssc.inc(inst)
                ctx.mark(f"xtev{j}", ssc)
        ctx.mark("xln_done", ssc)
        if KPH < 2:
            return
        for ip in range(NPIECE):
            for t4 in range(4):
                k = ip * 4 + t4
                ctx.w(E, f"v{ip}_{t4}")
                ci = gcol(ip) // 128 + t4
                ops.copy(V[:, ci, 0:64], pv[t4 % 2][:, 0:64])
                inst = ops.copy(V[:, ci, 65:129], pv[t4 % 2][:, 64:128])
                ssc.inc(inst)
                ctx.mark(f"vev{k}", ssc)
        ctx.w(E, "nm_back")
        inst = ops.activation(small[0:1, 0:8], small[0:1, 0:8], AF.Sqrt)
        ssc.inc(inst)
        ctx.mark("nm_sqrt", ssc)
        ctx.w(E, "negm_mm")
        inst = ops.copy(negm[:], pqk[0][0:128, 0:4])
        ssc.inc(inst)
        ctx.mark("negm_sb", ssc)

    def f_scalar2(ctx, E, ops, C, P2, B2):
        ssc = C["ssc"]
        sps = P2["sps"]
        gun = 0
        last_pT = {0: None, 1: None}
        for bsi in range(NBS):
            b, s = bsi // 16, bsi % 16
            nk = 4 * (s + 1)
            nblk = (nk + 7) // 8
            for blk in range(nblk):
                c0 = blk * 8
                cn = min(8, nk - c0)
                if last_pT[blk % 2] is not None:
                    ctx.w(E, last_pT[blk % 2])
                for jp in range(cn // 2):
                    for h in range(2):
                        ctx.w(E, f"su{gun}")
                        inst = ops.activation(
                            B2["pT"][h][blk % 2][:, 2 * jp:2 * jp + 2, :],
                            sps[gun % 2][:], AF.Exp,
                            bias=negm[:, 2 * b + h:2 * b + h + 1],
                            scale=0.125)
                        ssc.inc(inst)
                        ctx.mark(f"expu{gun}", ssc)
                        gun += 1
                last_pT[blk % 2] = f"avb{bsi}_{blk}"
        ctx.mark("ph2_sc", ssc)

    def f_scalar3(ctx, E, ops, C, bufs, P3):
        ssc = C["ssc"]
        tr3, gup = P3["tr3"], P3["gup"]
        gidx = 0
        for it in range(NS3 + 1):
            prev, cur = it - 1, it
            if prev >= 0:
                for g in range(6):
                    n = 4 if g < 5 else 2
                    ctx.w(E, f"hqtrg{prev}_{g}")
                    inst = ops.copy(
                        bufs["hqT"][:, 4 * g:4 * g + n, :],
                        tr3[gidx % 2][:, 0:n, :])
                    gidx += 1
                    ssc.inc(inst)
                    ctx.mark(f"hqevg{prev}_{g}", ssc)
            if cur < NS3:
                ctx.w(E, f"bn3_{cur}")
                inst = ops.activation(bufs["stdv3"][:], bufs["bn2b"][:, 1:2],
                                      AF.Sqrt, bias=epsb[:])
                ssc.inc(inst)
                ctx.mark(f"sqrt3_{cur}", ssc)
                for g in range(2):
                    ctx.w(E, f"xqtrg{cur}_{g}")
                    inst = ops.copy(bufs["xqT"][:, 4 * g:4 * g + 4, :],
                                    tr3[gidx % 2][:, :, :])
                    gidx += 1
                    ssc.inc(inst)
                    ctx.mark(f"xqevg{cur}_{g}", ssc)
                for ig in range(NIG):
                    ctx.w(E, f"gu{cur}_{ig}")
                    if ig >= 2:
                        ctx.w(E, f"hm{cur}_{ig - 2}")
                    inst = ops.activation(bufs["gs"][ig % 2][:],
                                          gup[ig % 2][:, 0:256], AF.Silu)
                    ssc.inc(inst)
                    ctx.mark(f"silu{cur}_{ig}", ssc)

    def f_vector(ctx, E, ops, C):
        svc = C["svc"]
        inst = ops.memset(epsb[:], EPS)
        svc.inc(inst)
        ctx.mark("epsb", svc)
        ops.memset(V[:, :, 64:65], 1.0)
        inst = ops.memset(V[:, :, 129:130], 1.0)
        svc.inc(inst)
        ctx.mark("v_ones", svc)
        for i, (wi, ti, P) in enumerate(WTILES):
            ctx.w(E, f"ld_w{i}")
            if i > 0:
                ctx.w(E, f"wq_used{i - 1}")
            if wi == 2:
                for hh in range(2):
                    inst = emit_quant(
                        ctx, ops, E, svc,
                        wslice[0:P, 1408 * hh:1408 * (hh + 1)],
                        wqrow[0:P, 1408 * hh:1408 * (hh + 1)], P, 1408, SCR0)
                    svc.inc(inst)
            else:
                inst = emit_quant(ctx, ops, E, svc, wslice[0:P, 0:D],
                                  wqrow[0:P, 0:D], P, D, SCR0)
                svc.inc(inst)
            ctx.mark(f"wq_q{i}", svc)
            ctx.mark(f"wsl_free{i}", svc)
        for s in range(NSTRIP):
            ctx.w(E, f"ld_x{s}")
            ops.bn_stats(bn12[:, 0:6], xbuf[s % 2][:, 0:512])
            inst = ops.bn_stats(bn12[:, 6:12], xbuf[s % 2][:, 512:1024])
            svc.inc(inst)
            ctx.wv(E, svc, svc.n)
            inst = ops.bn_aggr(bn2[:], bn12[:])
            svc.inc(inst)
            ctx.mark(f"bn{s}", svc)
            ctx.w(E, f"sqrt{s}")
            inst = ops.reciprocal(rstd[:], stdv[:])
            svc.inc(inst)
            ctx.wv(E, svc, svc.n)
            if s >= 2:
                ctx.w(E, f"xtev{(s - 2) * 8 + 7}")
            inst = ops.tensor_scalar(xln[s % 2][:], xbuf[s % 2][:],
                                     bn2[:, 0:1], rstd[:],
                                     ALU.subtract, ALU.mult)
            svc.inc(inst)
            ctx.mark(f"ln_apply{s}", svc)
        if KPH < 2:
            return
        for ip in range(NPIECE):
            g = gcol(ip)
            m = (g % T) // 512
            a, w = m % 4, m // 4
            Ct = ctab[32 * a:32 * a + 32, 512 * w:512 * w + 512]
            St = stab[32 * a:32 * a + 32, 512 * w:512 * w + 512]
            for cc in range(2):
                ctx.w(E, f"qk{ip}_{cc}")
                if cc == 0 and ip >= 2:
                    ctx.w(E, f"qdma{ip - 2}")
                dstq = qTb[ip % 2][:] if cc == 0 else kT[:, g:g + 512]
                src = pqk[cc]
                for h in range(2):
                    he = src[64 * h:64 * h + 32, :]
                    ho = src[64 * h + 32:64 * h + 64, :]
                    ops.tensor_tensor(rop[0][:], he, Ct, ALU.mult)
                    ops.tensor_tensor(rop[1][:], ho, St, ALU.mult)
                    ops.tensor_tensor(dstq[64 * h:64 * h + 32, :],
                                      rop[0][:], rop[1][:], ALU.subtract)
                    ops.tensor_tensor(rop[0][:], he, St, ALU.mult)
                    ops.tensor_tensor(rop[1][:], ho, Ct, ALU.mult)
                    inst = ops.tensor_tensor(dstq[64 * h + 32:64 * h + 64, :],
                                             rop[0][:], rop[1][:], ALU.add)
                svc.inc(inst)
                ctx.mark(f"rope{ip}_{cc}", svc)
                if ip + cc > 0:
                    pip, pcc = (ip, 0) if cc == 1 else (ip - 1, 1)
                    ctx.w(E, f"nmm{pip}_{pcc}")
                inst = ops.tensor_tensor(sqb[:], dstq[0:128, 0:512],
                                         dstq[0:128, 0:512], ALU.mult)
                svc.inc(inst)
                ctx.mark(f"sq{ip}_{cc}", svc)
            for cc in range(2):
                ctx.w(E, f"nmm{ip}_{cc}")
                dst = nmq if cc == 0 else nmk
                inst = ops.tensor_reduce(dst[:, ip:ip + 1], pn[0:2, :],
                                         AX.X, ALU.max)
                svc.inc(inst)
                ctx.mark(f"nred{ip}_{cc}", svc)
        ctx.wv(E, svc, svc.n)
        for cc, srcb in [(0, nmq), (1, nmk)]:
            for b in range(2):
                inst = ops.tensor_reduce(
                    nred[:, 2 * cc + b:2 * cc + b + 1, None],
                    srcb[:, :].rearrange("q (r p) -> q r p", p=4)
                    [:, :, 2 * b:2 * b + 2],
                    AX.XY, ALU.max)
                svc.inc(inst)
        ctx.mark("nred_done", svc)
        ctx.w(E, "nm_sqrt")
        inst = ops.tensor_tensor(small[0:1, 8:12], small[0:1, 0:4],
                                 small[0:1, 4:8], ALU.mult)
        svc.inc(inst)
        ctx.wv(E, svc, svc.n)
        inst = ops.tensor_scalar(small[0:1, 12:16], small[0:1, 8:12],
                                 -0.125, None, ALU.mult)
        svc.inc(inst)
        ctx.wv(E, svc, svc.n)
        inst = ops.tensor_copy(smbf[0:1, 0:4], small[0:1, 12:16])
        svc.inc(inst)
        ctx.mark("negm_bf", svc)

    def f_vector2(ctx, E, ops, C, P2, B2):
        svc = C["svc"]
        avp = P2["avp"]
        gun = 0
        for bsi in range(NBS):
            b, s = bsi // 16, bsi % 16
            nk = 4 * (s + 1)
            gun0 = gun
            gun += nk
            if bsi == 0:
                ctx.w(E, "consts")
            for kc in range(nk - 4, nk):
                blk = kc // 8
                j = kc % 8
                dd = kc - 4 * s
                ua = gun0 + (kc // 2) * 2
                ctx.w(E, f"expu{ua}")
                ctx.w(E, f"expu{ua + 1}")
                for h in range(2):
                    inst = ops.tensor_tensor(B2["pT"][h][blk % 2][:, j, :],
                                             B2["pT"][h][blk % 2][:, j, :],
                                             mask[:, dd, :], ALU.mult)
                svc.inc(inst)
                ctx.mark(f"mask{bsi}_{kc}", svc)
            ctx.w(E, f"avb{bsi}_{(nk - 1) // 8}")
            ops.reciprocal(B2["zif"][0:1, 0:512], avp[0][bsi % 2][64:65, :])
            inst = ops.reciprocal(B2["zif"][0:1, 512:1024],
                                  avp[1][bsi % 2][64:65, :])
            svc.inc(inst)
            ctx.wv(E, svc, svc.n)
            inst = ops.tensor_copy(B2["zib"][:], B2["zif"][:])
            svc.inc(inst)
            ctx.mark(f"zi{bsi}", svc)
            ctx.wv(E, C["sg_z"], 32 * (bsi + 1))
            if bsi >= 2:
                ctx.w(E, f"attdma{bsi - 2}")
            ops.tensor_tensor(B2["attb"][bsi % 2][:, 0:512],
                              avp[0][bsi % 2][0:64, :],
                              B2["zbs"][:, 0:512], ALU.mult)
            inst = ops.tensor_tensor(B2["attb"][bsi % 2][:, 512:1024],
                                     avp[1][bsi % 2][0:64, :],
                                     B2["zbs"][:, 512:1024], ALU.mult)
            svc.inc(inst)
            ctx.mark(f"attmul{bsi}", svc)
        ctx.mark("ph2_sv", svc)

    def f_vector3(ctx, E, ops, C, bufs, P3):
        svc = C["svc"]
        po, gup = P3["po"], P3["gup"]
        x1b, xbuf3, xln2 = bufs["x1"], bufs["xbuf3"], bufs["xln2"]
        xq, gs = bufs["xq"], bufs["gs"]
        h_sb, hq_sb = bufs["h_sb"], bufs["hq_sb"]
        SCR3 = bufs["SCR3"]
        for it in range(NS3 + 1):
            prev, cur = it - 1, it
            if cur < NS3:
                x1 = x1b[cur % 2]
                ctx.w(E, f"po{cur}")
                ctx.w(E, f"ld_x3_{cur}")
                for half in range(2):
                    inst = ops.tensor_tensor(x1[:, 512 * half:512 * (half + 1)],
                                             po[half][:],
                                             xbuf3[:, 512 * half:512 * (half + 1)],
                                             ALU.add)
                svc.inc(inst)
                ctx.mark(f"x1_{cur}", svc)
                ops.bn_stats(bufs["bn12b"][:, 0:6], x1[:, 0:512])
                inst = ops.bn_stats(bufs["bn12b"][:, 6:12], x1[:, 512:1024])
                svc.inc(inst)
                ctx.wv(E, svc, svc.n)
                inst = ops.bn_aggr(bufs["bn2b"][:], bufs["bn12b"][:])
                svc.inc(inst)
                ctx.mark(f"bn3_{cur}", svc)
                ctx.w(E, f"sqrt3_{cur}")
                inst = ops.reciprocal(bufs["rstd3"][:], bufs["stdv3"][:])
                svc.inc(inst)
                ctx.wv(E, svc, svc.n)
            if prev >= 0:
                ctx.w(E, f"down{prev}")
                if prev >= 1:
                    ctx.w(E, f"outdma{prev - 1}")
                ob = bufs["out_sb"][:, 0:2048].bitcast(F32)
                for half in range(2):
                    inst = ops.tensor_tensor(
                        ob[:, 512 * half:512 * (half + 1)],
                        P3["dn"][half][:],
                        x1b[prev % 2][:, 512 * half:512 * (half + 1)], ALU.add)
                svc.inc(inst)
                ctx.mark(f"outadd{prev}", svc)
            if cur < NS3:
                x1 = x1b[cur % 2]
                inst = ops.tensor_scalar(xln2[0:128, 0:D], x1[0:128, 0:D],
                                         bufs["bn2b"][:, 0:1],
                                         bufs["rstd3"][:], ALU.subtract,
                                         ALU.mult)
                svc.inc(inst)
                ctx.wv(E, svc, svc.n)
                if cur >= 1:
                    ctx.w(E, f"xqtrg{cur - 1}_1")
                inst = emit_quant(ctx, ops, E, svc, xln2[0:128, 0:D], xq[:],
                                  128, D, SCR3)
                svc.inc(inst)
                ctx.mark(f"xq{cur}", svc)
                for ig in range(NIG):
                    if ig == 0 and prev >= 0:
                        ctx.w(E, f"outdma{prev}")
                    ctx.w(E, f"silu{cur}_{ig}")
                    inst = ops.tensor_tensor(h_sb[:, IG * ig:IG * (ig + 1)],
                                             gup[ig % 2][:, 256:512],
                                             gs[ig % 2][:], ALU.mult)
                    svc.inc(inst)
                    ctx.mark(f"hm{cur}_{ig}", svc)
                if cur >= 1:
                    ctx.w(E, f"hqtrg{cur - 1}_5")
                for hh in range(2):
                    inst = emit_quant(ctx, ops, E, svc,
                                      h_sb[:, 1408 * hh:1408 * (hh + 1)],
                                      hq_sb[cur % 2][:, 1408 * hh:1408 * (hh + 1)],
                                      128, 1408, SCR3)
                    svc.inc(inst)
                ctx.mark(f"hq{cur}", svc)
        ctx.mark("ph3_sv", svc)

    def fix_ev(EV):
        EV["wsl_freeB0"] = ("svc", 0)
        for i in range(1, len(WTILES)):
            EV[f"wsl_freeB{i}"] = EV[f"wsl_free{i - 1}"]
        if KPH < 2:
            return
        if KPH < 3:
            return

    def mk_ctrs():
        return {n: Ctr(n) for n in semnames}

    mock_ops = MockOps()

    class _MockEng:
        def wait_ge(self, *a):
            pass

        def dma_start(self, **kw):
            return MockInst()

        def collective_compute(self, *a, **kw):
            return MockInst()

    MT = _MT()
    P2r = {"sps": MT, "avp": MT}
    P3r = {"po": MT, "dn": MT, "tr3": MT, "gup": MT}
    bufsr = {k: MT for k in
             ["att_t", "xbuf3", "x1", "xln2", "xq", "xqT", "gs", "h_sb",
              "hq_sb", "hqT", "wg_sb", "wu_sb", "wd_sb", "bn12b",
              "bn2b", "stdv3", "rstd3", "rs1b", "out_sb"]}
    bufsr["SCR3"] = {k: MT for k in ["xn", "sc", "iv", "c1", "c2", "c3", "c4"]}

    def f_tail(ctx, E, C, dummy):
        sg = C["sg"]
        ctx.w(E, "tail_ready")
        for so in range(NSTRIP):
            sg.inc(E.dma_start(out=out_h.ap()[128 * so:128 * (so + 1), :],
                               in_=dummy[:]), 16)
        ctx.wv(E, sg, sg.n)

    def f_tailv(ctx, ops, C):
        svc = C["svc"]
        svc.inc(ops.memset(xbuf[0][:], 0.0))
        ctx.mark("tail_ready", svc)

    B2r = {k: MT for k in ["qstrip", "pT", "zif", "zib", "zbs", "attb"]}
    ctxr = Ctx(True, EV)
    ctxr.C = mk_ctrs()
    mke = _MockEng()
    f_sync(ctxr, mke, ctxr.C)
    if KPH >= 2:
        f_sync2(ctxr, mke, ctxr.C, B2r)
    if KPH >= 3:
        f_sync3(ctxr, mke, ctxr.C, bufsr)
    f_gpsimd(ctxr, mke, ctxr.C)
    if KPH >= 2:
        f_gpsimd2(ctxr, mke, ctxr.C, B2r)
    if KPH >= 3:
        f_gpsimd3(ctxr, mke, ctxr.C, bufsr)
    f_tensor(ctxr, mke, mock_ops, ctxr.C)
    if KPH >= 2:
        f_tensor2(ctxr, mke, mock_ops, ctxr.C, P2r, B2r)
    if KPH >= 3:
        f_tensor3(ctxr, mke, mock_ops, ctxr.C, bufsr, P3r)
    f_scalar(ctxr, mke, mock_ops, ctxr.C)
    if KPH >= 2:
        f_scalar2(ctxr, mke, mock_ops, ctxr.C, P2r, B2r)
    if KPH >= 3:
        f_scalar3(ctxr, mke, mock_ops, ctxr.C, bufsr, P3r)
    f_vector(ctxr, mke, mock_ops, ctxr.C)
    if KPH >= 2:
        f_vector2(ctxr, mke, mock_ops, ctxr.C, P2r, B2r)
    if KPH >= 3:
        f_vector3(ctxr, mke, mock_ops, ctxr.C, bufsr, P3r)
    if KPH < 3:
        f_tailv(ctxr, mock_ops, ctxr.C)
        f_tail(ctxr, mke, ctxr.C, MT)
    fix_ev(EV)
    totals = {n: c.n for n, c in ctxr.C.items()}

    ctx = Ctx(False, EV)
    ctx.C = mk_ctrs()
    for n in semnames:
        ctx.C[n].sem = handles[n]

    with nc.Block() as block:
        @block.sync
        def _(sync):
            f_sync(ctx, sync, ctx.C)

        @block.gpsimd
        def _(gp):
            f_gpsimd(ctx, gp, ctx.C)

        @block.tensor
        def _(tn):
            f_tensor(ctx, tn, nc.tensor, ctx.C)

        @block.scalar
        def _(sc_):
            f_scalar(ctx, sc_, nc.scalar, ctx.C)

        @block.vector
        def _(ve):
            f_vector(ctx, ve, nc.vector, ctx.C)

        if KPH < 2:
            @block.vector
            def _(ve):
                f_tailv(ctx, nc.vector, ctx.C)

            @block.gpsimd
            def _(gp):
                f_tail(ctx, gp, ctx.C, xbuf[0])
        es_ps1.close()
        es0.close()
        if KPH < 2:
            for n in semnames:
                assert ctx.C[n].n == totals[n], (n, ctx.C[n].n, totals[n])
            es12.close()
            es3.close()
            esg.close()
            return nc
        B2 = {
            "qstrip": [sb(es12, f"qstrip{i}", [128, 512], BF16)
                       for i in range(2)],
            "pT": [[sb(es12, f"pT{h}_{i}", [128, 8, 512], BF16)
                    for i in range(2)] for h in range(2)],
            "zif": sb(es12, "zif", [1, 1024], F32),
            "zib": sb(es12, "zib", [1, 1024], BF16),
            "zbs": sb(es12, "zbs", [64, 1024], BF16),
            "attb": [sb(es12, f"attb{i}", [64, 1024], BF16)
                     for i in range(2)],
        }
        P2 = {
            "sps": [pst(es_ps2, f"sps{i}", [128, 1024]) for i in range(2)],
            "avp": [[pst(es_ps2, f"avp{h}_{i}", [65, 512]) for i in range(2)]
                    for h in range(2)],
        }

        @block.sync
        def _(sync):
            f_sync2(ctx, sync, ctx.C, B2)

        @block.gpsimd
        def _(gp):
            f_gpsimd2(ctx, gp, ctx.C, B2)

        @block.tensor
        def _(tn):
            f_tensor2(ctx, tn, nc.tensor, ctx.C, P2, B2)

        @block.scalar
        def _(sc_):
            f_scalar2(ctx, sc_, nc.scalar, ctx.C, P2, B2)

        @block.vector
        def _(ve):
            f_vector2(ctx, ve, nc.vector, ctx.C, P2, B2)

        es_ps2.close()
        if KPH < 3:
            @block.vector
            def _(ve):
                f_tailv(ctx, nc.vector, ctx.C)

            @block.gpsimd
            def _(gp):
                f_tail(ctx, gp, ctx.C, xbuf[0])
            for n in semnames:
                assert ctx.C[n].n == totals[n], (n, ctx.C[n].n, totals[n])
            es12.close()
            es3.close()
            esg.close()
            return nc
        es12.close()
        bufs = {}
        bufs["wg_sb"] = sb(es3, "wg_sb", [128, 8, II], BF16)
        bufs["wu_sb"] = sb(es3, "wu_sb", [128, 8, II], BF16)
        bufs["wd_sb"] = sb(es3, "wd_sb", [128, 22, D], BF16)
        bufs["att_t"] = sb(es3, "att_t", [128, 8, 128], BF16)
        bufs["xbuf3"] = sb(es3, "xbuf3", [128, D], F32)
        bufs["x1"] = [sb(es3, f"x1_{i}", [128, D], F32) for i in range(2)]
        bufs["rs1b"] = sb(es3, "rs1b", [128, 1], F32)
        bufs["xq"] = sb(es3, "xq", [128, D], BF16)
        bufs["xqT"] = sb(es3, "xqT", [128, 8, 128], BF16)
        bufs["gs"] = [sb(es3, f"gs{i}", [128, IG], BF16) for i in range(2)]
        bufs["h_sb"] = sb(es3, "h_sb", [128, II], BF16)
        bufs["hq_sb"] = [sb(es3, "hq_sb0", [128, II], BF16)] * 2
        bufs["hqT"] = sb(es3, "hqT", [128, 22, 128], BF16)
        bufs["bn12b"] = sb(es3, "bn12b", [128, 12], F32)
        bufs["bn2b"] = sb(es3, "bn2b", [128, 2], F32)
        bufs["stdv3"] = sb(es3, "stdv3", [128, 1], F32)
        bufs["rstd3"] = sb(es3, "rstd3", [128, 1], F32)
        QW3 = 1408
        q3 = {"xn": sb(es3, "q3xn", [128, QW3], F32),
              "sc": sb(es3, "q3sc", [128, QW3 // 16], F32),
              "iv": sb(es3, "q3iv", [128, QW3 // 16], F32)}
        for k in ["c1", "c2", "c3", "c4"]:
            q3[k] = sb(es3, f"q3{k}", [128, QW3], BF16)
        bufs["SCR3"] = q3
        bufs["xln2"] = q3["xn"]
        bufs["out_sb"] = bufs["h_sb"]
        P3 = {
            "po": [pst(es3, f"po{i}", [128, 512]) for i in range(2)],
            "dn": [pst(es3, f"dn{i}", [128, 512]) for i in range(2)],
            "tr3": [pst(es3, f"tr3{i}", [128, 4, 128], BF16) for i in range(2)],
            "gup": [pst(es3, f"gup{i}", [128, 512]) for i in range(2)],
        }

        @block.sync
        def _(sync):
            f_sync3(ctx, sync, ctx.C, bufs)

        @block.gpsimd
        def _(gp):
            f_gpsimd3(ctx, gp, ctx.C, bufs)

        @block.tensor
        def _(tn):
            f_tensor3(ctx, tn, nc.tensor, ctx.C, bufs, P3)

        @block.scalar
        def _(sc_):
            f_scalar3(ctx, sc_, nc.scalar, ctx.C, bufs, P3)

        @block.vector
        def _(ve):
            f_vector3(ctx, ve, nc.vector, ctx.C, bufs, P3)

    for n in semnames:
        assert ctx.C[n].n == totals[n], (n, ctx.C[n].n, totals[n])
    es3.close()
    esg.close()
    return nc


_NC_CACHE = {}


def get_nc(debug=False):
    key = bool(debug)
    if key not in _NC_CACHE:
        _NC_CACHE[key] = build_nc(key)
    return _NC_CACHE[key]


def make_inputs(x, qkv_w, qkv_b, out_w, out_b, wg, wu, wd):
    inv_freq = (1.0 / 10000.0 **
                (np.arange(0, HD, 2, dtype=np.float32) / np.float32(HD))
                ).astype(np.float32)
    t = np.arange(T, dtype=np.float32)
    freqs = (t[:, None] * inv_freq[None, :]).astype(np.float32)
    rope = np.concatenate([np.sin(freqs.astype(np.float64)),
                           np.cos(freqs.astype(np.float64))], -1
                          ).astype(np.float32)
    def pack_tab(tab):
        return np.ascontiguousarray(
            tab.reshape(32, 4, 4, 512).transpose(2, 0, 1, 3).reshape(128, 2048))
    S_tab = pack_tab(np.ascontiguousarray(rope[:, 0::2].T))
    C_tab = pack_tab(np.ascontiguousarray(rope[:, 1::2].T))
    kk = np.arange(128)[:, None]
    qq = np.arange(512)[None, :]
    mask01 = np.ascontiguousarray(
        np.stack([(qq >= kk + 128 * dd).astype(np.float32)
                  for dd in range(4)], axis=1))
    ident = np.eye(128, dtype=np.float32)
    e2m = np.zeros((128, 2), np.float32)
    e2m[0:64, 0] = 1.0
    e2m[64:128, 1] = 1.0
    onesm = np.ones((1, 512), np.float32)

    in_maps = []
    for c in range(NC):
        hA, hB = 2 * c, 2 * c + 1
        perm = []
        for sec in range(2):
            for hh in (hA, hB):
                base = sec * D + hh * HD
                perm += [base + j for j in range(0, HD, 2)]
                perm += [base + j for j in range(1, HD, 2)]
        vcols = [2 * D + hh * HD + j for hh in (hA, hB) for j in range(HD)]
        xo = np.ascontiguousarray(
            x[:, 1024 * c:1024 * (c + 1), :].reshape(TOK, D))
        m = {
            "x_own": xo,
            "wqk": np.ascontiguousarray(qkv_w[:, perm]),
            "wv": np.ascontiguousarray(qkv_w[:, vcols]),
            "bqk": np.ascontiguousarray(qkv_b[perm])[None, :],
            "bv": np.ascontiguousarray(qkv_b[vcols])[None, :],
            "wout": np.ascontiguousarray(out_w),
            "bout": np.ascontiguousarray(out_b)[None, :],
            "wg_sl": np.ascontiguousarray(wg[WGR * c:WGR * (c + 1), :]),
            "wu_sl": np.ascontiguousarray(wu[WGR * c:WGR * (c + 1), :]),
            "wd_sl": np.ascontiguousarray(wd[WDR * c:WDR * (c + 1), :]),
            "ctab": C_tab, "stab": S_tab, "mask01": mask01,
            "ident": ident, "e2": e2m, "ones_in": onesm,
        }
        in_maps.append(m)
    return in_maps


def kernel(L0_attn_x, L0_attn_qkv_w, L0_attn_qkv_b, L0_attn_out_w,
           L0_attn_out_b, L0_ffn_w_gate, L0_ffn_w_up, L0_ffn_w_down):
    debug = os.environ.get("KDBG", "") == "1"
    in_maps = make_inputs(np.asarray(L0_attn_x), np.asarray(L0_attn_qkv_w),
                          np.asarray(L0_attn_qkv_b), np.asarray(L0_attn_out_w),
                          np.asarray(L0_attn_out_b), np.asarray(L0_ffn_w_gate),
                          np.asarray(L0_ffn_w_up), np.asarray(L0_ffn_w_down))
    nc = get_nc(debug)
    res = run_bass_kernel_spmd(nc, in_maps, core_ids=list(range(NC)))
    out = np.empty((B, T, D), dtype=np.float32)
    for c in range(NC):
        r = np.asarray(res.results[c]["out"], dtype=np.float32)
        out[0, 1024 * c:1024 * (c + 1), :] = r[0:1024]
        out[1, 1024 * c:1024 * (c + 1), :] = r[1024:2048]
    if debug:
        kernel.dbg = [dict(res.results[c]) for c in range(NC)]
    return out
